# version 13
# baseline (speedup 1.0000x reference)
import numpy as np
import concourse.bass as bass
import concourse.mybir as mybir
from concourse.bass_utils import run_bass_kernel_spmd

F32 = mybir.dt.float32
BF16 = mybir.dt.bfloat16
FP8 = mybir.dt.float8e4
AF = mybir.ActivationFunctionType
ALU = mybir.AluOpType
AX = mybir.AxisListType

TS = 1024
NCH = 8
D = 2048
NKT = 16
H = 16
DK = 64
DV = 128
M = 64
FI = 6144
NCT = 48
SCALE = 0.125
EPS = 1e-5
KB = 1024

ENGS = ("sync", "scalar", "vector", "tensor", "gpsimd")


class Prog:

    def __init__(self, nc):
        self.nc = nc
        self.ops = []
        self.cnt = {}
        for e in ENGS:
            self.cnt[e + ".c"] = {"sem": nc.alloc_semaphore(f"cc_{e}"), "v": 0}
        self.keyval = {}
        self._last_gc = 0

    def _cnt(self, cname):
        if cname not in self.cnt:
            self.cnt[cname] = {"sem": self.nc.alloc_semaphore(cname), "v": 0}
        return self.cnt[cname]

    def op(self, eng, fn, waits=(), key=None, dma=False, ch=None):
        if dma:
            assert ch is not None, "DMA ops need a channel"
            cname = f"{eng}.q.{ch}"
        else:
            cname = eng + ".c"
        c = self._cnt(cname)
        wt = [w for w in waits if w is not None]
        if eng == "gpsimd" and not dma:
            if self._last_gc:
                wt.append(("gpsimd.c", self._last_gc))
        c["v"] += 16 if dma else 1
        if eng == "gpsimd" and not dma:
            self._last_gc = c["v"]
        if key is not None:
            assert key not in self.keyval, f"dup key {key}"
            self.keyval[key] = (cname, c["v"])
        for w in wt:
            if isinstance(w, str):
                assert w in self.keyval, f"unknown wait key {w}"
        self.ops.append((eng, cname, fn, tuple(wt), c["v"]))

    def emit(self, blk):
        by_eng = {e: [] for e in ENGS}
        for eng, cname, fn, waits, incv in self.ops:
            by_eng[eng].append((cname, fn, waits, incv))
        kv = self.keyval
        cnt = self.cnt

        def mk(ename):
            items = by_eng[ename]

            def body(eng):
                lw = {cn: 0 for cn in cnt}
                prev = {cn: 0 for cn in cnt}
                own = ename + ".c"
                ownv = 0
                for cname, fn, waits, incv in items:
                    if ename in ("vector", "scalar") and cname == own and ownv:
                        eng.wait_ge(cnt[own]["sem"], ownv)
                    for wk in waits:
                        wcn, wv = kv[wk] if isinstance(wk, str) else wk
                        weng = wcn.split(".")[0]
                        wkind = wcn.split(".")[1]
                        if (weng == ename and wkind == "c"
                                and ename != "gpsimd"):
                            continue
                        if lw[wcn] >= wv:
                            continue
                        eng.wait_ge(cnt[wcn]["sem"], wv)
                        lw[wcn] = wv
                    ins = fn(eng)
                    ins.then_inc(cnt[cname]["sem"], incv - prev[cname])
                    prev[cname] = incv
                    if cname == own:
                        ownv = incv
            return body

        for e in ENGS:
            if by_eng[e]:
                getattr(blk, e)(mk(e))


class VT:

    def __init__(self, arena, rowb, base, rows, cols, dtype):
        self.arena, self.rowb, self.base = arena, rowb, base
        self.rows, self.cols, self.dtype = rows, cols, dtype
        self.sz = mybir.dt.size(dtype)
        assert base % 32 == 0 and base + cols * self.sz <= rowb

    def __getitem__(self, idx):
        rs, cs = idx
        r0, r1, st = rs.indices(self.rows)
        c0, c1, st2 = cs.indices(self.cols)
        assert st == 1 and st2 == 1
        return bass.AP(self.arena, r0 * self.rowb + self.base + c0 * self.sz,
                       [[self.rowb, r1 - r0], [1, (c1 - c0) * self.sz]]
                       ).bitcast(self.dtype)


def A(buf, off, pattern):
    if not isinstance(buf, VT):
        return bass.AP(buf, off, pattern)
    rowlen, nrows = pattern[0]
    r0, col = divmod(off, rowlen)
    assert pattern[-1][0] == 1, "innermost dim must be unit-stride"
    fd = [[s * buf.sz, n] for s, n in pattern[1:-1]]
    fd.append([1, pattern[-1][1] * buf.sz])
    return bass.AP(buf.arena, r0 * buf.rowb + buf.base + col * buf.sz,
                   [[buf.rowb, nrows]] + fd).bitcast(buf.dtype)


def build(dbg=False, rd=False, upto="p8", p5c=NCH, p5g=4, p5s=10, dbgcc=False,
          dsel=None, ot_sites=("wod", "p6x", "p7")):
    if dsel is None:
        dsel = {"p0", "p1", "agg", "p5", "ot", "x2t", "ft"}

    def dg(grp):
        return dbg and grp in dsel
    _ph = ["consts", "p0", "p1", "p2", "p3", "p4", "p5", "p6", "p7", "p8"]
    _lim = _ph.index(upto)

    def _on(ph):
        return _ph.index(ph) <= _lim

    nc = bass.Bass("TRN2", target_bir_lowering=False,
                   detect_race_conditions=rd)

    x_in = nc.dram_tensor("x", [TS, D], F32, kind="ExternalInput").ap()
    wts = {}
    for nm, shp in [("wq", [D, 1024]), ("wk", [D, 1024]), ("ws", [D, 1024]),
                    ("wv", [D, 2048]), ("wg", [D, 2048]), ("wo", [D, 2048]),
                    ("gatew", [D, FI]), ("upw", [D, FI]), ("downw", [FI, D])]:
        wts[nm] = nc.dram_tensor(nm, shp, F32, kind="ExternalInput").ap()
    gnw = nc.dram_tensor("gnw", [DV], F32, kind="ExternalInput").ap()
    gates = nc.dram_tensor("gates", [FI], F32, kind="ExternalInput").ap()
    ups = nc.dram_tensor("ups", [FI], F32, kind="ExternalInput").ap()
    downs = nc.dram_tensor("downs", [D], F32, kind="ExternalInput").ap()
    pmask_d = nc.dram_tensor("pmask", [128, 4], F32, kind="ExternalInput").ap()

    out = nc.dram_tensor("out", [TS, D], F32, kind="ExternalOutput").ap()
    x1s = nc.dram_tensor("x1s", [TS, D], F32,
                         kind="ExternalOutput" if dbg else "Internal").ap()
    down8s = nc.dram_tensor("down8s", [FI, D], FP8).ap()
    AGG_KV = 1536
    AGG_S = 1024
    AGG_N = 128 * AGG_KV + AGG_S
    aggin = nc.dram_tensor("aggin", [AGG_N], F32).ap()
    aggout = nc.dram_tensor("aggout", [4 * AGG_N], F32).ap()

    dbgd = {}
    if dbg:
        for nm, shp, dt in [("d_xlnT", [128, 16384], BF16),
                            ("d_kT", [128, 8192], BF16),
                            ("d_eT", [128, 8192], BF16),
                            ("d_qT", [128, 8192], BF16),
                            ("d_v", [128, 16384], BF16),
                            ("d_silug", [128, 16384], BF16),
                            ("d_agg", [128, AGG_KV], F32),
                            ("d_aggs", [1, AGG_S], F32),
                            ("d_sS", [128, 512], F32),
                            ("d_okn", [128, 512], F32),
                            ("d_pvn", [128, 512], BF16),
                            ("d_of", [128, 512], BF16),
                            ("d_oT", [128, 16384], BF16),
                            ("d_x2T", [128, 16384], BF16),
                            ("d_xl2", [128, 16384], BF16),
                            ("d_kT2", [128, 8192], BF16),
                            ("d_scx", [128, 2048], F32),
                            ("d_scxln", [128, 2048], F32),
                            ("d_stat", [128, 32], F32),
                            ("d_fuseT", [128, 49152], BF16)]:
            dbgd[nm] = nc.dram_tensor(nm, shp, dt, kind="ExternalOutput").ap()

    def sb(name, shape, dtype, off):
        return nc.alloc_sbuf_tensor_at(name, shape, dtype, offset=off)

    R = 16512
    c_triu4 = sb("c_triu4", [128, 512], F32, R + 0)
    c_triu_bf = sb("c_triu_bf", [128, 128], BF16, R + 2048)
    c_id_bf = sb("c_id_bf", [128, 128], BF16, R + 2304)
    c_id_f32 = sb("c_id_f32", [128, 128], F32, R + 2560)
    c_ones_bf = sb("c_ones_bf", [128, 16], BF16, R + 3072)
    c_ones_f32 = sb("c_ones_f32", [1, 128], F32, R + 3104)
    c_ones_bfr = sb("c_ones_bfr", [1, 128], BF16, R + 26752)
    st_s_bf = sb("st_s_bf", [1, AGG_S], BF16, R + 27008)
    c_gs = sb("c_gs", [128, 48], F32, R + 3616)
    c_us = sb("c_us", [128, 48], F32, R + 3808)
    c_gnwb = sb("c_gnwb", [128, 128], F32, R + 4000)
    c_pm = sb("c_pm", [128, 4], F32, R + 4512)
    c_bias = sb("c_bias", [128, 4], F32, R + 4544)
    c_dsb = sb("c_dsb", [128, 2048], F32, R + 4608)
    c_gnwrep = sb("c_gnwrep", [1, 128], F32, R + 12800)
    st_kv = sb("st_kv", [128, 1536], F32, R + 13312)
    st_kv_bf = sb("st_kv_bf", [128, 1536], BF16, R + 19456)
    st_s = sb("st_s", [1, AGG_S], F32, R + 22528)
    sc_stat = sb("sc_stat", [128, 32], F32, R + 26624)

    SA = R + 29056
    BBROW = 147456
    sbA = sb("arenaA", [128, 25088], mybir.dt.uint8, SA)
    sbB = sb("arenaB", [128, BBROW], mybir.dt.uint8, SA + 25088)

    def vA(base, rows, cols, dtype):
        return VT(sbA, 25088, base, rows, cols, dtype)

    def vB(base, rows, cols, dtype):
        return VT(sbB, BBROW, base, rows, cols, dtype)

    sc_e = vA(0, 128, 1024, BF16)
    sc_k = vA(2048, 128, 1024, BF16)
    sc_at = vA(4096, 128, 1024, BF16)
    sc_wt = vA(6144, 128, 1024, BF16)
    sc_pvt = vA(8192, 128, 512, BF16)
    sc_sS = vA(9216, 128, 512, F32)
    sc_sinv = vA(11264, 128, 512, F32)
    sc_okn = vA(13312, 128, 512, F32)
    sc_pv = vA(15360, 128, 512, F32)
    sc_pvn = vA(17408, 128, 512, BF16)
    sc_osq = vA(18432, 128, 512, F32)
    sc_of = vA(20480, 128, 512, BF16)
    sc_h1 = vA(21504, 128, 16, F32)
    sc_h2 = vA(21568, 128, 16, F32)
    sc_h3 = vA(21632, 128, 16, F32)
    sc_h4 = vA(21696, 128, 16, F32)
    sc_h5 = vA(21760, 128, 16, F32)
    sc_h6 = vA(21824, 128, 16, F32)
    sc_x = [vA(i * 8192, 128, 2048, F32) for i in range(2)]
    sc_xln = vA(16384, 128, 2048, F32)
    c_dsrep = vA(16384, 1, 2048, F32)
    ring_f = [vA(i * 2048, 128, 512, F32) for i in range(2)]
    ring_b = [vA(4096 + i * 1024, 128, 512, BF16) for i in range(4)]
    sw_b = [vA(8192 + i * 1024, 128, 512, BF16) for i in range(2)]
    sw_f = [vA(10240 + i * 2048, 128, 512, F32) for i in range(2)]
    pf_kv = vA(14848, 128, AGG_KV, F32)
    c_gnwb4 = vA(14848, 128, 512, F32)
    pf_s = vA(20992, 1, AGG_S, F32)
    r8 = [vA(i * 2048, 128, 2048, FP8) for i in range(4)]
    sg_sc = vA(8192, 128, 512, F32)
    p8_x1 = [vA(10240 + i * 2048, 128, 512, F32) for i in range(2)]
    p8_o = [vA(14336 + i * 2048, 128, 512, F32) for i in range(2)]

    xlnT = vB(0, 128, 16384, BF16)
    kT = vB(32 * KB, 128, 8192, BF16)
    eT = vB(48 * KB, 128, 8192, BF16)
    v_nat = vB(64 * KB, 128, 16384, BF16)
    qT = vB(96 * KB, 128, 8192, BF16)
    silug = vB(112 * KB, 128, 16384, BF16)
    oT = vB(0, 128, 16384, BF16)
    p6_x = [vB(32 * KB + i * 4096, 128, 1024, F32) for i in range(4)]
    p6_x1 = [vB(48 * KB + i * 8192, 128, 2048, F32) for i in range(4)]
    wo_ring = [vB(80 * KB + i * 2048, 128, 1024, BF16) for i in range(4)]
    x2T = vB(96 * KB, 128, 16384, BF16)
    fuseT = vB(0, 128, 49152, BF16)
    xln2a = vB(32 * KB, 128, 2048, F32)
    xln2b = vB(88 * KB, 128, 2048, F32)
    down8 = vB(96 * KB, 128, 24576, FP8)

    PB = [nc.alloc_psum_tensor(f"pb{i}", [128, 512], F32) for i in range(8)]
    BREL = {b: None for b in range(8)}
    SREL = {("b", 0): None, ("b", 1): None, ("f", 0): None, ("f", 1): None}

    p = Prog(nc)
    O = p.op

    def _noop(eng, fn, waits=(), key=None, dma=False, ch=None):
        if key is not None:
            p.keyval[key] = ("tensor.c", 0)

    def _gate(ph):
        return p.op if _on(ph) else _noop


    O("sync", lambda e: e.dma_start(
        out=c_gs[:, :], in_=A(gates.tensor, 0, [[1, 128], [128, 48]]),
        allow_slow_non_contiguous=True),
      key="ld_gs", dma=True, ch="cst")
    O("sync", lambda e: e.dma_start(
        out=c_us[:, :], in_=A(ups.tensor, 0, [[1, 128], [128, 48]]),
        allow_slow_non_contiguous=True),
      waits=("ld_gs",), key="ld_us", dma=True, ch="cst")
    O("sync", lambda e: e.dma_start(
        out=c_gnwrep[0:1, :],
        in_=A(gnw.tensor, 0, [[128, 1], [1, 128]])),
      waits=("ld_us",), key="ld_gnw", dma=True, ch="cst")
    O("sync", lambda e: e.dma_start(out=c_pm[:, :], in_=pmask_d),
      waits=("ld_gnw",), key="ld_pm", dma=True, ch="cst")

    O("gpsimd", lambda e: e.memset(c_bias[:, 0:1], -32.0))
    O("gpsimd", lambda e: e.memset(c_bias[:, 1:2], float(EPS)), key="bias_c")
    O("gpsimd", lambda e: e.memset(c_triu4[:, :], 1.0))
    O("gpsimd", lambda e: e.affine_select(
        out=c_triu4[:, :].rearrange("q (r f) -> q r f", f=128),
        in_=c_triu4[:, :].rearrange("q (r f) -> q r f", f=128),
        pattern=[[0, 4], [1, 128]], compare_op=ALU.is_ge, fill=0.0,
        base=0, channel_multiplier=-1))
    O("gpsimd", lambda e: e.memset(c_triu_bf[:, :], 1.0))
    O("gpsimd", lambda e: e.affine_select(
        out=c_triu_bf[:, :], in_=c_triu_bf[:, :], pattern=[[1, 128]],
        compare_op=ALU.is_ge, fill=0.0, base=0, channel_multiplier=-1))
    O("gpsimd", lambda e: e.memset(c_id_bf[:, :], 1.0))
    O("gpsimd", lambda e: e.affine_select(
        out=c_id_bf[:, :], in_=c_id_bf[:, :], pattern=[[1, 128]],
        compare_op=ALU.is_equal, fill=0.0, base=0, channel_multiplier=-1))
    O("gpsimd", lambda e: e.memset(c_id_f32[:, :], 1.0))
    O("gpsimd", lambda e: e.affine_select(
        out=c_id_f32[:, :], in_=c_id_f32[:, :], pattern=[[1, 128]],
        compare_op=ALU.is_equal, fill=0.0, base=0, channel_multiplier=-1))
    O("gpsimd", lambda e: e.memset(c_ones_bf[:, :], 1.0))
    O("gpsimd", lambda e: e.memset(c_ones_f32[:, :], 1.0))
    O("gpsimd", lambda e: e.memset(st_kv[:, :], 0.0))
    O("gpsimd", lambda e: e.memset(st_kv_bf[:, :], 0.0))
    O("gpsimd", lambda e: e.memset(c_ones_bfr[:, :], 1.0))
    O("gpsimd", lambda e: e.memset(st_s[:, :], 0.0))
    O("gpsimd", lambda e: e.memset(sc_stat[:, :], 0.0))
    O("gpsimd", lambda e: e.memset(st_s_bf[:, :], 0.0), key="consts_g")
    O("tensor", lambda e: e.matmul(PB[0][:, 0:128], c_ones_f32[0:1, :],
                                   c_gnwrep[0:1, :], start=True, stop=True),
      waits=("consts_g", "ld_gnw"), key="mm_gnwb")
    O("vector", lambda e: e.tensor_copy(c_gnwb[:, :], PB[0][:, 0:128]),
      waits=("mm_gnwb",), key="ev_gnwb")
    BREL[0] = "ev_gnwb"
    def emit_ln_chunk(c, tag, src_key, src_ap, dstT, dstT_cols, prev_gate,
                      xbuf=None):
        sb8 = 8 * (c % 2)
        st = sc_stat[:, sb8:sb8 + 8]
        xb = xbuf if xbuf is not None else sc_xln
        O("vector", lambda e: e.reduce_sum(st[:, 0:1], src_ap, axis=AX.X),
          waits=(src_key,), key=f"{tag}rs{c}")
        O("scalar", lambda e: e.activation(xb[:, :], src_ap, AF.Square,
                                           accum_out=st[:, 1:2]),
          waits=(src_key, prev_gate), key=f"{tag}sq{c}")
        O("vector", lambda e: e.tensor_scalar(
            st[:, 0:1], st[:, 0:1], 1.0 / D, None, ALU.mult),
          key=f"{tag}mn{c}")
        O("vector", lambda e: e.tensor_tensor(
            st[:, 2:3], st[:, 0:1], st[:, 0:1], ALU.mult))
        O("vector", lambda e: e.scalar_tensor_tensor(
            st[:, 3:4], st[:, 1:2], 1.0 / D, st[:, 2:3],
            ALU.mult, ALU.subtract),
          waits=(f"{tag}sq{c}",), key=f"{tag}var{c}")
        O("scalar", lambda e: e.activation(st[:, 4:5], st[:, 3:4], AF.Ln,
                                           bias=c_bias[:, 1:2]),
          waits=(f"{tag}var{c}", "bias_c"))
        O("scalar", lambda e: e.activation(st[:, 5:6], st[:, 4:5], AF.Exp,
                                           scale=-0.5), key=f"{tag}rst{c}")
        O("vector", lambda e: e.tensor_scalar(
            xb[:, :], src_ap, st[:, 0:1], st[:, 5:6],
            ALU.subtract, ALU.mult),
          waits=(f"{tag}rst{c}", prev_gate), key=f"{tag}ap{c}")
        evks = []
        for ktg in range(4):
            b = ktg
            for j in range(4):
                kt = ktg * 4 + j
                wl = [f"{tag}ap{c}"] if j == 0 else []
                if j == 0 and BREL[b]:
                    wl.append(BREL[b])
                O("tensor", lambda e, kt=kt, b=b, j=j, xb=xb: e.transpose(
                    PB[b][:, j * 128:(j + 1) * 128],
                    xb[:, kt * 128:(kt + 1) * 128], c_id_f32[:, :]),
                  waits=wl, key=f"{tag}tp{c}_{ktg}" if j == 3 else None)
            ek = f"{tag}ev{c}_{ktg}"
            for j in range(4):
                O("vector", lambda e, ktg=ktg, b=b, j=j: e.tensor_copy(
                    A(dstT, (ktg * 4 + j) * dstT_cols + c * 128,
                      [[16384, 128], [1, 128]]),
                    PB[b][:, j * 128:(j + 1) * 128]),
                  waits=(f"{tag}tp{c}_{ktg}",) if j == 0 else (),
                  key=ek if j == 3 else None)
            BREL[b] = ek
            evks.append(ek)
        return evks

    O = _gate("p0")
    xl_keys = {}
    for c in range(NCH):
        O("sync", lambda e, c=c: e.dma_start(
            out=sc_x[c % 2][:, :], in_=x_in[c * 128:(c + 1) * 128, :]),
          waits=((f"P0ap{c-2}",) if c >= 2 else ()), key=f"x0_{c}",
          dma=True, ch=f"x{c % 2}")
        prev_gate = f"P0ev{c-2}_3" if c >= 2 else "ev_gnwb"
        evks = emit_ln_chunk(c, "P0", f"x0_{c}", sc_x[c % 2][:, :],
                             xlnT, 1024, prev_gate,
                             xbuf=(sc_xln if c % 2 == 0 else xln2a))
        for ktg in range(4):
            xl_keys[(c, ktg)] = evks[ktg]

    if dg("p0"):
        O("sync", lambda e: e.dma_start(out=dbgd["d_xl2"], in_=xlnT[:, :]),
          waits=(xl_keys[(NCH - 1, 3)],), dma=True, ch="dbg3", key="dmp_xl2")
        O("sync", lambda e: e.dma_start(out=dbgd["d_scx"], in_=sc_x[1][:, :]),
          waits=("dmp_xl2",), dma=True, ch="dbg3", key="dmp_scx")
        O("sync", lambda e: e.dma_start(out=dbgd["d_scxln"], in_=sc_xln[:, :]),
          waits=("dmp_scx",), dma=True, ch="dbg3", key="dmp_scxln")
        O("sync", lambda e: e.dma_start(out=dbgd["d_stat"], in_=sc_stat[:, :]),
          waits=("dmp_scxln",), dma=True, ch="dbg3", key="dmp_stat")

    sw = {"i": 0}

    def emit_sweep(w_ap, ncols, tag, evac, first_wait=None, tout=False):
        nng = ncols // 512
        j0 = sw["i"]
        for ng in range(nng):
            mmlast = None
            for kt in range(NKT):
                i = sw["i"]
                sw["i"] += 1
                qeng = "sync" if (i % 2 == 0) else "scalar"
                wld = [f"swcv_{i-2}"] if i >= 2 else []
                if first_wait and i - j0 < 2:
                    wld.append(first_wait)
                O(qeng, lambda e, kt=kt, ng=ng, i=i, w=w_ap: e.dma_start(
                    out=ring_f[i % 2][:, :],
                    in_=w[kt * 128:(kt + 1) * 128, ng * 512:(ng + 1) * 512]),
                  waits=wld, key=f"swd_{i}", dma=True, ch="sw")
                O("vector", lambda e, i=i: e.tensor_copy(
                    ring_b[i % 4][:, :], ring_f[i % 2][:, :]),
                  waits=(f"swd_{i}",) + ((f"swmm_{i-4}",) if i >= 4 else ()),
                  key=f"swcv_{i}")
                for s8i in range(NCH):
                    wl = [f"swcv_{i}"]
                    if kt == 0:
                        wl.append(BREL[s8i])
                    last = (s8i == NCH - 1)
                    if tout:
                        p2, th = s8i // 2, s8i % 2
                        wl.append(xl_keys[(4 * th + 3, kt // 4)])
                        O("tensor", lambda e, p2=p2, th=th, kt=kt, i=i,
                          b=s8i: e.matmul(
                            PB[b][:, :],
                            bass.AP(ring_b[i % 4].arena,
                                    ring_b[i % 4].base + p2 * 256,
                                    [[25088, 128], [1, 256]]).bitcast(BF16),
                            A(xlnT, kt * 1024 + th * 512,
                              [[16384, 128], [1, 512]]),
                            start=(kt == 0), stop=(kt == NKT - 1)),
                          waits=wl, key=f"swmm_{i}" if last else None)
                    else:
                        wl.append(xl_keys[(s8i, kt // 4)])
                        O("tensor", lambda e, c=s8i, kt=kt, i=i: e.matmul(
                            PB[c][:, :],
                            A(xlnT, kt * 1024 + c * 128,
                                    [[16384, 128], [1, 128]]),
                            ring_b[i % 4][:, :],
                            start=(kt == 0), stop=(kt == NKT - 1)),
                          waits=wl, key=f"swmm_{i}" if last else None)
                    if last:
                        mmlast = f"swmm_{i}"
            for s8i in range(NCH):
                BREL[s8i] = evac(s8i, ng, mmlast)

    def ev_transposed(dstT, tag):
        def evac(slot, ng, mmkey):
            p2, th = slot // 2, slot % 2
            pr = ng * 4 + p2
            O("vector", lambda e, pr=pr, th=th, slot=slot: e.tensor_copy(
                A(dstT, pr * 1024 + th * 512, [[8192, 128], [1, 512]]),
                PB[slot][:, :]),
              waits=(mmkey,), key=f"{tag}_{pr}_{th}")
            return f"{tag}_{pr}_{th}"
        return evac

    def ev_s(slot, ng, mmkey):
        p2, th = slot // 2, slot % 2
        pr = ng * 4 + p2
        fb = sw_f[slot % 2]
        O("vector", lambda e, slot=slot, fb=fb: e.tensor_scalar(
            fb[:, :], PB[slot][:, :], -32.0, 32.0, ALU.max, ALU.min),
          waits=(mmkey, SREL[("f", slot % 2)]),
          key=f"Pscl_{pr}_{th}")
        SREL[("f", slot % 2)] = f"Pse_{pr}_{th}"
        O("scalar", lambda e, pr=pr, th=th, fb=fb: e.activation(
            A(eT, pr * 1024 + th * 512, [[8192, 128], [1, 512]]),
            fb[:, :], AF.Exp, bias=c_bias[:, 0:1]),
          waits=(f"Pscl_{pr}_{th}", "bias_c"), key=f"Pse_{pr}_{th}")
        return f"Pse_{pr}_{th}"

    def ev_plain(dst, tag):
        def evac(c, ng, mmkey):
            O("vector", lambda e, c=c, ng=ng: e.tensor_copy(
                A(dst, c * 2048 + ng * 512, [[16384, 128], [1, 512]]),
                PB[c][:, :]),
              waits=(mmkey,), key=f"{tag}ev{c}_{ng}")
            return f"{tag}ev{c}_{ng}"
        return evac

    def ev_g(c, ng, mmkey):
        O("scalar", lambda e, c=c: e.activation(
            sg_sc[:, :], PB[c][:, :], AF.Silu),
          waits=(mmkey,) + ((f"Pgev{c-1}_{ng}" if c >= 1 else
                             (f"Pgev{NCH-1}_{ng-1}" if ng >= 1 else None),)),
          key=f"Pgsl{c}_{ng}")
        O("vector", lambda e, c=c, ng=ng: e.tensor_tensor(
            A(silug, c * 2048 + ng * 512, [[16384, 128], [1, 512]]),
            sg_sc[:, :], c_gnwb4[:, :], ALU.mult),
          waits=(f"Pgsl{c}_{ng}", "gnwb4"), key=f"Pgev{c}_{ng}")
        return f"Pgev{c}_{ng}"

    O = _gate("p1")
    emit_sweep(wts["wk"], 1024, "Pk", ev_transposed(kT, "Pkt"),
               first_wait="dmp_xl2" if dg("p0") else None, tout=True)
    if dg("p0"):
        O("sync", lambda e: e.dma_start(out=dbgd["d_kT2"], in_=kT[:, :]),
          waits=("Pkt_7_1",), dma=True, ch="dbg3", key="dmp_kT2")
    emit_sweep(wts["ws"], 1024, "Ps", ev_s,
               first_wait="dmp_kT2" if dg("p0") else None, tout=True)
    emit_sweep(wts["wv"], 2048, "Pv", ev_plain(v_nat, "Pv"))

    O = _gate("p2")
    def emit_ek_derive(c, tag, bnk_e, bnk_k, wait_extra=()):
        assert bnk_e != bnk_k
        keys = {}
        for half in range(2):
            for j in range(4):
                pr = half * 4 + j
                wl = list(wait_extra)
                wl.append(f"Pse_{pr}_{c // 4}")
                if j == 0:
                    wl.append(BREL[bnk_e])
                O("tensor", lambda e, pr=pr, j=j, c=c: e.transpose(
                    PB[bnk_e][:, :].bitcast(BF16)[:, j * 128:(j + 1) * 128],
                    A(eT, pr * 1024 + c * 128, [[8192, 128], [1, 128]]),
                    c_id_bf[:, :]),
                  waits=wl, key=f"{tag}et{c}_{half}" if j == 3 else None)
                wl2 = [f"Pkt_{pr}_{c // 4}"]
                if j == 0:
                    wl2.append(BREL[bnk_k])
                O("tensor", lambda e, pr=pr, j=j, c=c: e.transpose(
                    PB[bnk_k][:, :].bitcast(BF16)[:, j * 128:(j + 1) * 128],
                    A(kT, pr * 1024 + c * 128, [[8192, 128], [1, 128]]),
                    c_id_bf[:, :]),
                  waits=wl2, key=f"{tag}kt{c}_{half}" if j == 3 else None)
            ke = f"{tag}ee{c}_{half}"
            O("vector", lambda e, half=half, c=c: e.tensor_copy(
                sc_e[:, half * 512:(half + 1) * 512],
                PB[bnk_e][:, :].bitcast(BF16)[:, 0:512]),
              waits=(f"{tag}et{c}_{half}",), key=ke)
            BREL[bnk_e] = ke
            kk = f"{tag}ke{c}_{half}"
            O("vector", lambda e, half=half, c=c: e.tensor_copy(
                sc_k[:, half * 512:(half + 1) * 512],
                PB[bnk_k][:, :].bitcast(BF16)[:, 0:512]),
              waits=(f"{tag}kt{c}_{half}",), key=kk)
            BREL[bnk_k] = kk
            keys[half] = (ke, kk)
        return keys

    O("gpsimd", lambda e: e.memset(pf_s[:, :], 0.0),
      waits=(f"P0tp{NCH-1}_3",), key="pfs0")
    O("gpsimd", lambda e: e.memset(pf_kv[:, :], 0.0),
      waits=("pfs0",), key="pfkv0")
    for c in range(NCH):
        dk = emit_ek_derive(c, f"A{c}", 3, 4)
        for h in range(H):
            pr, par = h // 2, h % 2
            tp = (0, 64) if par else None
            half = h // 8
            wl = [dk[half][0], dk[half][1]]
            if h == 0:
                wl += [BREL[0], BREL[1], BREL[2], BREL[6], BREL[7]]
            O("tensor", lambda e, h=h, pr=pr, par=par, tp=tp: e.matmul(
                PB[0][par * 64:par * 64 + 64, pr * 64:(pr + 1) * 64],
                sc_k[:, h * 64:(h + 1) * 64], sc_e[:, h * 64:(h + 1) * 64],
                start=True, stop=True, tile_position=tp),
              waits=wl, key=f"Ahk{c}_{h}")
            bv = 1 + pr // 4
            O("tensor", lambda e, h=h, pr=pr, par=par, tp=tp, c=c, bv=bv:
              e.matmul(
                PB[bv][par * 64:par * 64 + 64,
                       (pr % 4) * 128:(pr % 4 + 1) * 128],
                sc_e[:, h * 64:(h + 1) * 64],
                A(v_nat, c * 2048 + h * 128, [[16384, 128], [1, 128]]),
                start=True, stop=True, tile_position=tp),
              waits=(f"Pvev{c}_{h // 4}",), key=f"Ahv{c}_{h}")
            bs = 6 + h // 8
            O("tensor", lambda e, h=h, bs=bs: e.matmul(
                PB[bs][0:1, (h % 8) * 64:(h % 8 + 1) * 64],
                c_ones_bf[:, 0:1], sc_e[:, h * 64:(h + 1) * 64],
                start=True, stop=True),
              key=f"Ahs{c}_{h}")
        for pr in range(8):
            O("vector", lambda e, pr=pr: e.tensor_tensor(
                A(pf_kv, pr * 192, [[1536, 128], [1, 64]]),
                A(pf_kv, pr * 192, [[1536, 128], [1, 64]]),
                PB[0][:, pr * 64:(pr + 1) * 64], ALU.add),
              waits=(f"Ahk{c}_{H-1}", "pfkv0") if pr == 0 else (),
              key=f"Aak{c}" if pr == 7 else None)
        BREL[0] = f"Aak{c}"
        for b in (1, 2):
            for j in range(4):
                pr = (b - 1) * 4 + j
                O("vector", lambda e, b=b, j=j, pr=pr: e.tensor_tensor(
                    A(pf_kv, 64 + pr * 192, [[1536, 128], [1, 128]]),
                    A(pf_kv, 64 + pr * 192, [[1536, 128], [1, 128]]),
                    PB[b][:, j * 128:(j + 1) * 128], ALU.add),
                  waits=(f"Ahv{c}_{H-1}",) if j == 0 else (),
                  key=f"Aav{c}_{b}" if j == 3 else None)
            BREL[b] = f"Aav{c}_{b}"
        for b in (6, 7):
            O("vector", lambda e, b=b: e.tensor_tensor(
                pf_s[0:1, (b - 6) * 512:(b - 5) * 512],
                pf_s[0:1, (b - 6) * 512:(b - 5) * 512],
                PB[b][0:1, :], ALU.add),
              waits=(f"Ahs{c}_{H-1}", "pfs0"), key=f"Aas{c}_{b}")
            BREL[b] = f"Aas{c}_{b}"
    p.keyval["Aevk"] = p.keyval[f"Aak{NCH-1}"]
    p.keyval["Aevv1"] = p.keyval[f"Aav{NCH-1}_1"]
    p.keyval["Aevv2"] = p.keyval[f"Aav{NCH-1}_2"]
    p.keyval["Aevs6"] = p.keyval[f"Aas{NCH-1}_6"]
    p.keyval["Aevs7"] = p.keyval[f"Aas{NCH-1}_7"]
    O("sync", lambda e: e.dma_start(
        out=A(aggin.tensor, 0, [[1536, 128], [1, 1536]]),
        in_=pf_kv[:, :]), waits=("Aevk", "Aevv1", "Aevv2"),
      key="Adkv", dma=True, ch="agg")
    O("sync", lambda e: e.dma_start(
        out=A(aggin.tensor, 128 * 1536, [[1024, 1], [1, 1024]]),
        in_=pf_s[0:1, :]), waits=("Aevs6", "Aevs7", "Adkv"),
      key="Ads", dma=True, ch="agg")
    if dg("agg"):
        O("sync", lambda e: e.dma_start(out=dbgd["d_agg"], in_=pf_kv[:, :]),
          waits=("Adkv",), dma=True, ch="dbg2", key="dmp_d_agg")
        O("sync", lambda e: e.dma_start(out=dbgd["d_aggs"], in_=pf_s[0:1, :]),
          waits=("Ads",), dma=True, ch="dbg2", key="dmp_d_aggs")
    if (not dbg) or dbgcc:
        O("gpsimd", lambda e: e.collective_compute(
            "AllGather", ALU.bypass,
            replica_groups=[[0, 1, 2, 3], [4, 5, 6, 7]],
            ins=[aggin.opt()], outs=[aggout.opt()]),
          waits=("Adkv", "Ads"), key="cc")

    O = _gate("p3")
    for j in range(4):
        O("vector", lambda e, j=j: e.tensor_copy(
            c_gnwb4[:, j * 128:(j + 1) * 128], c_gnwb[:, :]),
          waits=(("Adkv", "ev_gnwb")
                 + (("dmp_d_agg", "dmp_d_aggs") if dg("agg") else ()))
          if j == 0 else (),
          key="gnwb4" if j == 3 else None)
    emit_sweep(wts["wq"], 1024, "Pq", ev_transposed(qT, "Pqt"),
               first_wait=f"Ahs{NCH-1}_{H-1}", tout=True)
    emit_sweep(wts["wg"], 2048, "Pg", ev_g)

    O = _gate("p4")
    if (not dbg) or dbgcc:
        for r in range(4):
            O("gpsimd", lambda e, r=r: e.dma_start(
                out=pf_kv[:, :],
                in_=A(aggout.tensor, r * AGG_N,
                            [[1536, 128], [1, 1536]])),
              waits=("cc",) + ((f"P4kv{r-1}",) if r else
                               ("Adkv", f"Pgev{NCH-1}_3")),
              key=f"P4dkv{r}", dma=True, ch="p4a")
            O("vector", lambda e, r=r: e.scalar_tensor_tensor(
                st_kv[:, :], pf_kv[:, :], c_pm[:, r:r + 1], st_kv[:, :],
                ALU.mult, ALU.add),
              waits=(f"P4dkv{r}", "ld_pm"), key=f"P4kv{r}")
            O("gpsimd", lambda e, r=r: e.dma_start(
                out=pf_s[0:1, :],
                in_=A(aggout.tensor, r * AGG_N + 128 * 1536,
                            [[1024, 1], [1, 1024]])),
              waits=("cc",) + ((f"P4s{r-1}",) if r else ("Ads",)),
              key=f"P4ds{r}", dma=True, ch="p4b")
            O("vector", lambda e, r=r: e.scalar_tensor_tensor(
                st_s[0:1, :], pf_s[0:1, :], c_pm[0:1, r:r + 1],
                st_s[0:1, :], ALU.mult, ALU.add),
              waits=(f"P4ds{r}",), key=f"P4s{r}")
        O("vector", lambda e: e.tensor_copy(st_kv_bf[:, :], st_kv[:, :]),
          waits=("P4kv3",), key="P4bf")
        O("vector", lambda e: e.tensor_copy(st_s_bf[0:1, :], st_s[0:1, :]),
          waits=("P4s3",), key="P4sb")
        st_ready = ["P4bf", "P4sb"]
    else:
        st_ready = ["consts_g"]

    for ct in range(NCT):
        d8w = ()
        if ct == 0:
            if st_ready[0] == "P4bf":
                d8w = ("P4sb",)
            elif _on("p3"):
                d8w = (f"Pgev{NCH-1}_3",)
        O("gpsimd", lambda e, ct=ct: e.dma_start(
            out=down8s[ct * 128:(ct + 1) * 128, :],
            in_=wts["downw"][ct * 128:(ct + 1) * 128, :]),
          waits=d8w, key=f"d8_{ct}", dma=True, ch="d8")

    if dg("p1"):
        for nm, buf, wk in [("d_xlnT", xlnT, xl_keys[(NCH - 1, 3)]),
                            ("d_kT", kT, "Pkt_7_1"),
                            ("d_eT", eT, "Pse_7_1"),
                            ("d_qT", qT, "Pqt_7_1"),
                            ("d_v", v_nat, f"Pvev{NCH-1}_3"),
                            ("d_silug", silug, f"Pgev{NCH-1}_3")]:
            O("sync", lambda e, nm=nm, buf=buf: e.dma_start(
                out=dbgd[nm], in_=buf[:, :]), waits=(wk,), dma=True,
              ch="dbg2", key=f"dmp_{nm}")

    O = _gate("p5")
    stkey = {("kv", pr): st_ready[0] for pr in range(8)}
    for pr in range(8):
        stkey[("s", pr)] = st_ready[-1]
    last_snew = [None]

    for c in range(min(NCH, p5c)):
        O = _gate("p5")
        dk = emit_ek_derive(c, f"E{c}", 3, 4)
        ek_e = [dk[0][0], dk[1][0]]
        ek_k = [dk[0][1], dk[1][1]]
        dbg_gate = []
        for G in range(min(2, p5g)):
            hs = [8 * G + i for i in range(8)]
            prs = [4 * G + j for j in range(4)]
            tg = f"c{c}G{G}"
            O = p.op if (_lim >= 9 or p5s >= 1) and _on("p5") else _noop
            for i, h in enumerate(hs):
                pr, par = h // 2, h % 2
                bk = 0 if par == 0 else 4
                wl = [f"Pkt_{pr}_{c // 4}", f"Pqt_{pr}_{c // 4}"]
                if i == 0:
                    wl += [BREL[0], BREL[4]]
                O("tensor", lambda e, pr=pr, par=par, i=i, c=c, bk=bk:
                  e.matmul(
                    PB[bk][:, (i // 2) * 128:(i // 2 + 1) * 128],
                    A(kT, pr * 1024 + c * 128 + par * 64 * 8192,
                      [[8192, 64], [1, 128]]),
                    A(qT, pr * 1024 + c * 128 + par * 64 * 8192,
                      [[8192, 64], [1, 128]]),
                    start=True, stop=True),
                  waits=wl, key=f"{tg}at" if i == 7 else None)
            O = p.op if (_lim >= 9 or p5s >= 1.5) and _on("p5") else _noop
            for i in range(8):
                bk = 0 if i % 2 == 0 else 4
                O("vector", lambda e, i=i, bk=bk: e.tensor_tensor(
                    A(sc_at, i * 128, [[1024, 128], [1, 128]]),
                    PB[bk][:, (i // 2) * 128:(i // 2 + 1) * 128],
                    c_triu4[:, 0:128], ALU.mult),
                  waits=(f"{tg}at",) if i == 0 else (),
                  key=f"{tg}atm" if i == 7 else None)
            BREL[0] = f"{tg}atm"
            BREL[4] = f"{tg}atm"
            O = p.op if (_lim >= 9 or p5s >= 2) and _on("p5") else _noop
            for i, h in enumerate(hs):
                wl = [ek_e[h // 8], stkey[("s", h // 2)]]
                if i == 0:
                    wl.append(BREL[1])
                O("tensor", lambda e, h=h, i=i: e.matmul(
                    PB[1][:, i * 64:(i + 1) * 64], c_triu_bf[:, :],
                    sc_e[:, h * 64:(h + 1) * 64], start=True, stop=False),
                  waits=wl)
                O("tensor", lambda e, h=h, i=i: e.matmul(
                    PB[1][:, i * 64:(i + 1) * 64], c_ones_bfr[0:1, :],
                    st_s_bf[0:1, h * 64:(h + 1) * 64], start=False,
                    stop=True),
                  key=f"{tg}cs" if i == 7 else None)
            O("vector", lambda e: e.tensor_copy(sc_sS[:, :], PB[1][:, :]),
              waits=(f"{tg}cs", last_snew[0]) + tuple(dbg_gate),
              key=f"{tg}sev")
            BREL[1] = f"{tg}sev"
            O("vector", lambda e: e.reciprocal(sc_sinv[:, :], sc_sS[:, :]),
              key=f"{tg}sinv")
            O("gpsimd", lambda e, G=G: e.dma_start(
                out=st_s_bf[0:1, G * 512:(G + 1) * 512],
                in_=sc_sS[127:128, :]),
              waits=(f"{tg}sev",), key=f"{tg}snew", dma=True, ch=f"sn{G}")
            last_snew[0] = f"{tg}snew"
            for pr in prs:
                stkey[("s", pr)] = f"{tg}snew"
            O = p.op if (_lim >= 9 or p5s >= 3) and _on("p5") else _noop
            for i, h in enumerate(hs):
                pr, par = h // 2, h % 2
                wl = [f"{tg}atm"]
                if i == 0:
                    wl.append(BREL[2])
                    wl.append(stkey[("kv", prs[0])])
                O("tensor", lambda e, h=h, i=i: e.matmul(
                    PB[2][:, i * 64:(i + 1) * 64],
                    A(sc_at, i * 128, [[1024, 128], [1, 128]]),
                    sc_e[:, h * 64:(h + 1) * 64], start=True, stop=False),
                  waits=wl)
                O("tensor", lambda e, h=h, pr=pr, par=par, i=i, c=c: e.matmul(
                    PB[2][:, i * 64:(i + 1) * 64],
                    A(qT, pr * 1024 + c * 128 + par * 64 * 8192,
                      [[8192, 64], [1, 128]]),
                    A(st_kv_bf, pr * 192 + par * 64 * 1536,
                      [[1536, 64], [1, 64]]),
                    start=False, stop=True),
                  key=f"{tg}ok" if i == 7 else None)
            O("vector", lambda e: e.scalar_tensor_tensor(
                sc_okn[:, :], PB[2][:, :], SCALE, sc_sinv[:, :],
                ALU.mult, ALU.mult),
              waits=(f"{tg}ok", f"{tg}sinv"), key=f"{tg}okn")
            BREL[2] = f"{tg}okn"
            O = p.op if (_lim >= 9 or p5s >= 4) and _on("p5") else _noop
            O("vector", lambda e: e.reduce_max(
                sc_h1[:, 0:8],
                sc_okn[:, :].rearrange("q (r f) -> q r f", f=64), axis=AX.X),
              key=f"{tg}mx")
            O("vector", lambda e: e.tensor_scalar(
                sc_h2[:, 0:8], sc_h1[:, 0:8], -1.0, None, ALU.mult),
              key=f"{tg}nm")
            for i in range(8):
                O("scalar", lambda e, i=i: e.activation(
                    sc_pv[:, i * 64:(i + 1) * 64],
                    sc_okn[:, i * 64:(i + 1) * 64], AF.Exp,
                    bias=sc_h2[:, i:i + 1],
                    accum_out=sc_h3[:, i:i + 1]),
                  waits=(f"{tg}nm",), key=f"{tg}ex{i}")
            O("vector", lambda e: e.reciprocal(sc_h4[:, 0:8], sc_h3[:, 0:8]),
              waits=(f"{tg}ex7",), key=f"{tg}rs")
            for i in range(8):
                O("vector", lambda e, i=i: e.scalar_tensor_tensor(
                    sc_pvn[:, i * 64:(i + 1) * 64],
                    sc_pv[:, i * 64:(i + 1) * 64], sc_h4[:, i:i + 1],
                    sc_sinv[:, i * 64:(i + 1) * 64], ALU.mult, ALU.mult),
                  waits=(f"{tg}rs",), key=f"{tg}pvn" if i == 7 else None)
            O = p.op if (_lim >= 9 or p5s >= 5) and _on("p5") else _noop
            for j in range(4):
                wl = [f"{tg}pvn"]
                if j == 0:
                    wl.append(BREL[3])
                O("tensor", lambda e, j=j: e.transpose(
                    PB[3][:, :].bitcast(BF16)[:, j * 128:(j + 1) * 128],
                    sc_pvn[:, j * 128:(j + 1) * 128], c_id_bf[:, :]),
                  waits=wl, key=f"{tg}pt" if j == 3 else None)
            O("vector", lambda e: e.tensor_copy(
                sc_pvt[:, :], PB[3][:, :].bitcast(BF16)[:, 0:512]),
              waits=(f"{tg}pt",), key=f"{tg}pte")
            BREL[3] = f"{tg}pte"
            O = p.op if (_lim >= 9 or p5s >= 6) and _on("p5") else _noop
            for i, h in enumerate(hs):
                pr, par = h // 2, h % 2
                j = i // 2
                bk = 4 if par == 0 else 0
                wl = [f"{tg}pte"]
                if i == 0:
                    wl += [BREL[4], BREL[0]]
                O("tensor", lambda e, pr=pr, par=par, i=i, j=j, c=c,
                  bk=bk: e.matmul(
                    PB[bk][:, (i // 2) * 128:(i // 2 + 1) * 128],
                    A(eT, pr * 1024 + c * 128 + par * 64 * 8192,
                      [[8192, 64], [1, 128]]),
                    A(sc_pvt, j * 128 + par * 64 * 512,
                      [[512, 64], [1, 128]]),
                    start=True, stop=True),
                  waits=wl, key=f"{tg}wt" if i == 7 else None)
            for i in range(8):
                bk = 4 if i % 2 == 0 else 0
                O("vector", lambda e, i=i, bk=bk: e.tensor_tensor(
                    A(sc_wt, i * 128, [[1024, 128], [1, 128]]),
                    PB[bk][:, (i // 2) * 128:(i // 2 + 1) * 128],
                    c_triu4[:, 0:128], ALU.mult),
                  waits=(f"{tg}wt",) if i == 0 else (),
                  key=f"{tg}wtm" if i == 7 else None)
            BREL[4] = f"{tg}wtm"
            BREL[0] = f"{tg}wtm"
            for vh in range(2):
                O = p.op if (_lim >= 9 or p5s >= 7) and _on("p5") else _noop
                for i4 in range(4):
                    i = vh * 4 + i4
                    h = hs[i]
                    pr, par = h // 2, h % 2
                    j = i // 2
                    wl = [f"{tg}wtm", f"Pvev{c}_{h // 4}"]
                    if i4 == 0:
                        wl.append(BREL[5])
                    O("tensor", lambda e, h=h, i=i, i4=i4, c=c: e.matmul(
                        PB[5][:, i4 * 128:(i4 + 1) * 128],
                        A(sc_wt, i * 128, [[1024, 128], [1, 128]]),
                        A(v_nat, c * 2048 + h * 128, [[16384, 128], [1, 128]]),
                        start=True, stop=False),
                      waits=wl)
                    O("tensor", lambda e, h=h, pr=pr, par=par, i=i, i4=i4,
                      j=j: e.matmul(
                        PB[5][:, i4 * 128:(i4 + 1) * 128],
                        A(sc_pvt, j * 128 + par * 64 * 512,
                          [[512, 64], [1, 128]]),
                        A(st_kv_bf, pr * 192 + 64 + par * 64 * 1536,
                          [[1536, 64], [1, 128]]),
                        start=False, stop=True),
                      key=f"{tg}ov{vh}" if i4 == 3 else None)
                O = p.op if (_lim >= 9 or p5s >= 8) and _on("p5") else _noop
                for i4 in range(4):
                    O("scalar", lambda e, i4=i4: e.activation(
                        sc_osq[:, i4 * 128:(i4 + 1) * 128],
                        PB[5][:, i4 * 128:(i4 + 1) * 128], AF.Square,
                        accum_out=sc_h5[:, i4:i4 + 1]),
                      waits=(f"{tg}ov{vh}",), key=f"{tg}sm{vh}_{i4}")
                O("scalar", lambda e: e.activation(
                    sc_h6[:, 0:4], sc_h5[:, 0:4], AF.Ln, scale=1.0 / DV,
                    bias=c_bias[:, 1:2]), waits=(f"{tg}sm{vh}_3", "bias_c"))
                O("scalar", lambda e: e.activation(
                    sc_h6[:, 4:8], sc_h6[:, 0:4], AF.Exp, scale=-0.5),
                  key=f"{tg}rstd{vh}")
                for i4 in range(4):
                    h = hs[vh * 4 + i4]
                    O("vector", lambda e, i4=i4, h=h, c=c:
                      e.scalar_tensor_tensor(
                        sc_of[:, i4 * 128:(i4 + 1) * 128],
                        PB[5][:, i4 * 128:(i4 + 1) * 128],
                        sc_h6[:, 4 + i4:5 + i4],
                        A(silug, c * 2048 + h * 128, [[16384, 128], [1, 128]]),
                        ALU.mult, ALU.mult),
                      waits=(f"{tg}rstd{vh}", f"Pgev{c}_{h // 4}"),
                      key=f"{tg}of{vh}" if i4 == 3 else None)
                BREL[5] = f"{tg}of{vh}"
                O = p.op if (_lim >= 9 or p5s >= 9) and _on("p5") else _noop
                for i4 in range(4):
                    wl = [f"{tg}of{vh}"]
                    if i4 == 0:
                        wl.append(BREL[3] if vh == 0 else f"{tg}ote0")
                    O("tensor", lambda e, i4=i4, vh=vh: e.transpose(
                        PB[3][:, :].bitcast(BF16)
                        [:, (vh * 4 + i4) * 128:(vh * 4 + i4 + 1) * 128],
                        sc_of[:, i4 * 128:(i4 + 1) * 128], c_id_bf[:, :]),
                      waits=wl, key=f"{tg}ot{vh}" if i4 == 3 else None)
                for i4 in range(4):
                    h = hs[vh * 4 + i4]
                    wlo = [f"{tg}ot{vh}"] if i4 == 0 else []
                    if vh == 0 and i4 == 0 and dg("p1") and c == 0 and G == 0:
                        wlo.append("dmp_d_xlnT")
                    O("vector", lambda e, h=h, c=c, i4=i4, vh=vh:
                      e.tensor_copy(
                        A(oT, h * 1024 + c * 128, [[16384, 128], [1, 128]]),
                        PB[3][:, :].bitcast(BF16)
                        [:, (vh * 4 + i4) * 128:(vh * 4 + i4 + 1) * 128]),
                      waits=wlo,
                      key=f"{tg}ote{vh}" if i4 == 3 else None)
            BREL[3] = f"{tg}ote1"
            O = p.op if (_lim >= 9 or p5s >= 10) and _on("p5") else _noop
            for i, h in enumerate(hs):
                pr, par = h // 2, h % 2
                j = i // 2
                tp = (0, 64) if par else None
                wl = [ek_e[h // 8], ek_k[h // 8]]
                if i == 0:
                    wl.append(BREL[6])
                O("tensor", lambda e, h=h, j=j, tp=tp, par=par: e.matmul(
                    PB[6][par * 64:par * 64 + 64, j * 64:(j + 1) * 64],
                    sc_k[:, h * 64:(h + 1) * 64],
                    sc_e[:, h * 64:(h + 1) * 64],
                    start=True, stop=True, tile_position=tp),
                  waits=wl, key=f"{tg}hk" if i == 7 else None)
                wl2 = [f"Pvev{c}_{h // 4}"]
                if i == 0:
                    wl2.append(BREL[7])
                O("tensor", lambda e, h=h, j=j, tp=tp, par=par, c=c: e.matmul(
                    PB[7][par * 64:par * 64 + 64, j * 128:(j + 1) * 128],
                    sc_e[:, h * 64:(h + 1) * 64],
                    A(v_nat, c * 2048 + h * 128, [[16384, 128], [1, 128]]),
                    start=True, stop=True, tile_position=tp),
                  waits=wl2, key=f"{tg}hv" if i == 7 else None)
            for j in range(4):
                O("vector", lambda e, G=G, j=j: e.tensor_tensor(
                    A(st_kv, (4 * G + j) * 192, [[1536, 128], [1, 64]]),
                    A(st_kv, (4 * G + j) * 192, [[1536, 128], [1, 64]]),
                    PB[6][:, j * 64:(j + 1) * 64], ALU.add),
                  waits=(f"{tg}hk", stkey[("kv", prs[0])]) if j == 0 else (),
                  key=f"{tg}sk" if j == 3 else None)
            for j in range(4):
                O("vector", lambda e, G=G, j=j: e.tensor_tensor(
                    A(st_kv, (4 * G + j) * 192 + 64, [[1536, 128], [1, 128]]),
                    A(st_kv, (4 * G + j) * 192 + 64, [[1536, 128], [1, 128]]),
                    PB[7][:, j * 128:(j + 1) * 128], ALU.add),
                  waits=(f"{tg}hv",) if j == 0 else (),
                  key=f"{tg}sv" if j == 3 else None)
            BREL[6] = f"{tg}sk"
            BREL[7] = f"{tg}sv"
            O("vector", lambda e, G=G: e.tensor_copy(
                A(st_kv_bf, 4 * G * 192, [[1536, 128], [1, 768]]),
                A(st_kv, 4 * G * 192, [[1536, 128], [1, 768]])),
              key=f"{tg}sbf")
            for pr in prs:
                stkey[("kv", pr)] = f"{tg}sbf"
            if dg("p5") and c == 0 and G == 0:
                prevd = None
                for nm, buf, wk in [("d_sS", sc_sS, "c0G0sev"),
                                    ("d_okn", sc_okn, "c0G0okn"),
                                    ("d_pvn", sc_pvn, "c0G0pvn"),
                                    ("d_of", sc_of, "c0G0of1")]:
                    O("sync", lambda e, nm=nm, buf=buf: e.dma_start(
                        out=dbgd[nm], in_=buf[:, :]), waits=(wk, prevd),
                      key=f"dmp_{nm}", dma=True, ch="dbg")
                    prevd = f"dmp_{nm}"
                dbg_gate = [f"dmp_{nm}" for nm in
                            ("d_sS", "d_okn", "d_pvn", "d_of")]

    if dg("ot") and p5c == NCH and p5g >= 2:
        O("sync", lambda e: e.dma_start(out=dbgd["d_oT"], in_=oT[:, :]),
          waits=(f"c{NCH-1}G1ote1",), dma=True, ch="dbg2", key="dmp_d_oT")
    elif dg("ot"):
        p.keyval["dmp_d_oT"] = ("tensor.c", 0)

    O = _gate("p6")
    ld_ds_w = [f"P0tp{NCH-1}_3"]
    if (not dbg) or dbgcc:
        ld_ds_w += ["P4s3", "P4bf"]
    if _on("p5") and p5c == NCH and p5g >= 2 and p5s >= 8:
        ld_ds_w.append(f"c{NCH-1}G1of1")
    O("sync", lambda e: e.dma_start(
        out=c_dsrep[0:1, :],
        in_=A(downs.tensor, 0, [[2048, 1], [1, 2048]])),
      waits=tuple(ld_ds_w), key="ld_ds", dma=True, ch="cst")
    for j in range(4):
        O("tensor", lambda e, j=j: e.matmul(
            PB[0][:, :], c_ones_f32[0:1, :],
            c_dsrep[0:1, j * 512:(j + 1) * 512], start=True, stop=True),
          waits=("ld_ds", BREL[0]), key=f"dsb{j}")
        O("vector", lambda e, j=j: e.tensor_copy(
            c_dsb[:, j * 512:(j + 1) * 512], PB[0][:, :]),
          waits=(f"dsb{j}",), key=f"dsbe{j}")
        BREL[0] = f"dsbe{j}"

    woi = {"i": 0}
    p6ev = {}
    for cg in range(2):
        for nh in range(2):
            for kt in range(NKT):
                i = woi["i"]
                woi["i"] += 1
                O("gpsimd", lambda e, kt=kt, nh=nh, i=i: e.dma_start(
                    out=wo_ring[i % 4][:, :],
                    in_=wts["wo"][kt * 128:(kt + 1) * 128,
                                  nh * 1024:(nh + 1) * 1024]),
                  waits=(f"womm_{i-4}",) if i >= 4 else
                  ((f"c{NCH-1}G1ote1",)
                   + (("dmp_d_oT",) if dg("ot") and "wod" in ot_sites
                      else ())),
                  key=f"wod_{i}", dma=True, ch=f"wo{i % 4}")
                for ci in range(4):
                    for n2 in range(2):
                        b = ci * 2 + n2
                        ch = cg * 4 + ci
                        wl = [f"wod_{i}", f"c{ch}G1ote1"]
                        if kt == 0:
                            wl.append(BREL[b])
                        last = (ci == 3 and n2 == 1)
                        O("tensor", lambda e, ch=ch, kt=kt, n2=n2, b=b, i=i:
                          e.matmul(
                            PB[b][:, :],
                            A(oT, kt * 1024 + ch * 128,
                                    [[16384, 128], [1, 128]]),
                            wo_ring[i % 4][:, n2 * 512:(n2 + 1) * 512],
                            start=(kt == 0), stop=(kt == NKT - 1)),
                          waits=wl, key=f"womm_{i}" if last else None)
            mmk = f"womm_{woi['i'] - 1}"
            for ci in range(4):
                ch = cg * 4 + ci
                wlx = [f"c{NCH-1}G1ote1"]
                if dg("ot") and "p6x" in ot_sites:
                    wlx.append("dmp_d_oT")
                if nh == 1:
                    wlx.append(f"p6a{cg}_0_{ci}_1")
                if cg:
                    wlx += [f"p6w{cg-1}_3", f"P6Lap{(cg-1)*4+ci}"]
                O("sync", lambda e, ch=ch, ci=ci, nh=nh: e.dma_start(
                    out=p6_x[ci][:, :],
                    in_=x_in[ch * 128:(ch + 1) * 128,
                             nh * 1024:(nh + 1) * 1024]),
                  waits=wlx, key=f"p6x{cg}_{nh}_{ci}", dma=True,
                  ch=f"p6x{ci}")
                for n2 in range(2):
                    b = ci * 2 + n2
                    ek = f"p6a{cg}_{nh}_{ci}_{n2}"
                    O("vector", lambda e, ci=ci, nh=nh, n2=n2, b=b: e.tensor_tensor(
                        p6_x1[ci][:, nh * 1024 + n2 * 512:
                                  nh * 1024 + (n2 + 1) * 512],
                        p6_x[ci][:, n2 * 512:(n2 + 1) * 512],
                        PB[b][:, :], ALU.add),
                      waits=(mmk, f"p6x{cg}_{nh}_{ci}"), key=ek)
                    BREL[b] = ek
        for ci in range(4):
            ch = cg * 4 + ci
            src_key = f"p6a{cg}_1_{ci}_1"
            O("sync", lambda e, ci=ci, ch=ch: e.dma_start(
                out=x1s[ch * 128:(ch + 1) * 128, :], in_=p6_x1[ci][:, :]),
              waits=(src_key, f"p6a{cg}_0_{ci}_1"), key=f"p6w{cg}_{ci}",
              dma=True, ch="p6w")
            prev_gate = f"P6Lev{ch-2}_3" if ch >= 2 else "dsbe3"
            emit_ln_chunk(ch, "P6L", src_key, p6_x1[ci][:, :], x2T, 1024,
                          prev_gate,
                          xbuf=(sc_xln if ch % 2 == 0 else xln2b))
    if dg("x2t"):
        O("sync", lambda e: e.dma_start(out=dbgd["d_x2T"], in_=x2T[:, :]),
          waits=(f"P6Lev{NCH-1}_3",), dma=True, ch="dbg2", key="dmp_d_x2T")

    O = _gate("p7")
    x2k = f"P6Lev{NCH-1}_3"
    for ct in range(NCT):
        pb0 = 4 * (ct % 2)
        for w_ap, rj in ((wts["gatew"], 0), (wts["upw"], 1)):
            O("gpsimd", lambda e, ct=ct, w=w_ap, rj=rj: e.dma_start(
                out=r8[2 * (ct % 2) + rj][:, :],
                in_=A(w.tensor, ct * 128,
                            [[FI, 128], [FI * 128, 16], [1, 128]])),
              waits=(f"P7f{ct-2}_1",) if ct >= 2 else (x2k,),
              key=f"P7d{ct}_{rj}", dma=True, ch=f"p7{2 * (ct % 2) + rj}")
        for tg in range(2):
            for rj, nm in ((0, "g"), (1, "u")):
                b = pb0 + rj * 2 + tg
                for kt in range(NKT):
                    wl = [f"P7d{ct}_{rj}", x2k]
                    if kt == 0:
                        wl.append(BREL[b])
                    O("tensor", lambda e, ct=ct, kt=kt, tg=tg, rj=rj, b=b:
                      e.matmul(
                        PB[b][:, :],
                        A(r8[2 * (ct % 2) + rj], kt * 128,
                                [[2048, 128], [1, 128]]),
                        A(x2T, kt * 1024 + tg * 512,
                                [[16384, 128], [1, 512]]),
                        start=(kt == 0), stop=(kt == NKT - 1)),
                      waits=wl,
                      key=f"P7m{ct}_{tg}_{nm}" if kt == NKT - 1 else None)
        for tg in range(2):
            bg = pb0 + 0 * 2 + tg
            bu = pb0 + 1 * 2 + tg
            O("scalar", lambda e, ct=ct, bg=bg: e.activation(
                sg_sc[:, :], PB[bg][:, :], AF.Silu, scale=c_gs[:, ct:ct + 1]),
              waits=(f"P7m{ct}_{tg}_g", "ld_gs")
              + ((f"P7f{ct}_0",) if tg == 1 else
                 ((f"P7f{ct-1}_1",) if ct >= 1 else ())),
              key=f"P7s{ct}_{tg}")
            ek = f"P7f{ct}_{tg}"
            O("vector", lambda e, ct=ct, tg=tg, bu=bu: e.scalar_tensor_tensor(
                A(fuseT, ct * 1024 + tg * 512, [[49152, 128], [1, 512]]),
                PB[bu][:, :], c_us[:, ct:ct + 1], sg_sc[:, :],
                ALU.mult, ALU.mult),
              waits=(f"P7m{ct}_{tg}_u", f"P7s{ct}_{tg}", "ld_us")
              + (("dmp_d_oT",) if dg("ot") and ct == 0
                 and "p7" in ot_sites else ()), key=ek)
            BREL[pb0 + tg] = ek
            BREL[pb0 + 2 + tg] = ek
    if dg("ft"):
        O("sync", lambda e: e.dma_start(out=dbgd["d_fuseT"], in_=fuseT[:, :]),
          waits=(f"P7f{NCT-1}_1",), dma=True, ch="dbg2")

    O = _gate("p8")
    for q in range(4):
        O("scalar", lambda e, q=q: e.dma_start(
            out=down8[:, :].rearrange("p (r f) -> p r f", f=512),
            in_=A(down8s.tensor, q * 512,
                        [[2048, 128], [2048 * 128, 48], [1, 512]])),
          waits=(f"d8_{NCT-1}",)
          + ((f"P8o{q-1}_{NCH-1}",) if q >= 1 else
             (f"P7m{NCT-1}_1_u", f"P7f{NCT-1}_1"))
          + (("dmp_d_x2T",) if dg("x2t") and q == 0 else ()),
          key=f"P8d{q}", dma=True, ch="p8d")
        for tc in range(NCH):
            b = tc % 2
            wlx = ["p6w1_3"]
            if tc >= 2:
                wlx.append(f"P8a{q}_{tc-2}")
            elif q >= 1:
                wlx.append(f"P8a{q-1}_{NCH-2+tc}")
            O("sync", lambda e, q=q, tc=tc: e.dma_start(
                out=p8_x1[tc % 2][:, :],
                in_=x1s[tc * 128:(tc + 1) * 128, q * 512:(q + 1) * 512]),
              waits=wlx, key=f"P8x{q}_{tc}", dma=True, ch=f"p8x{tc % 2}")
            for ct in range(NCT):
                wl = [f"P8d{q}", f"P7f{ct}_{0 if tc < 4 else 1}"]
                if ct == 0:
                    wl.append(BREL[b])
                O("tensor", lambda e, q=q, tc=tc, ct=ct, b=b: e.matmul(
                    PB[b][:, :],
                    A(fuseT, ct * 1024 + tc * 128,
                            [[49152, 128], [1, 128]]),
                    A(down8, ct * 512, [[24576, 128], [1, 512]]),
                    start=(ct == 0), stop=(ct == NCT - 1)),
                  waits=wl, key=f"P8m{q}_{tc}" if ct == NCT - 1 else None)
            wls = [f"P8m{q}_{tc}", "dsbe3"]
            if dg("agg") and q == 0 and tc < 2:
                wls.append("dmp_d_agg")
            if tc >= 2:
                wls.append(f"P8o{q}_{tc-2}")
            elif q >= 1:
                wls.append(f"P8o{q-1}_{NCH-2+tc}")
            O("vector", lambda e, q=q, tc=tc, b=b: e.tensor_tensor(
                p8_o[tc % 2][:, :], PB[b][:, :],
                c_dsb[:, q * 512:(q + 1) * 512], ALU.mult),
              waits=wls, key=f"P8s{q}_{tc}")
            BREL[b] = f"P8s{q}_{tc}"
            O("vector", lambda e, q=q, tc=tc: e.tensor_tensor(
                p8_o[tc % 2][:, :], p8_o[tc % 2][:, :], p8_x1[tc % 2][:, :],
                ALU.add),
              waits=(f"P8x{q}_{tc}",), key=f"P8a{q}_{tc}")
            O("sync", lambda e, q=q, tc=tc: e.dma_start(
                out=out[tc * 128:(tc + 1) * 128, q * 512:(q + 1) * 512],
                in_=p8_o[tc % 2][:, :]),
              waits=(f"P8a{q}_{tc}",), key=f"P8o{q}_{tc}", dma=True,
              ch=f"p8o{tc % 2}")

    with nc.Block("main") as blk:
        p.emit(blk)

        @blk.sync
        def _(e):
            for cn, c in p.cnt.items():
                if cn.startswith("sync.q"):
                    e.wait_ge(c["sem"], c["v"])

        @blk.gpsimd
        def _(e):
            for cn, c in p.cnt.items():
                if cn.startswith("gpsimd"):
                    e.wait_ge(c["sem"], c["v"])

        @blk.scalar
        def _(e):
            for cn, c in p.cnt.items():
                if cn.startswith("scalar.q"):
                    e.wait_ge(c["sem"], c["v"])

    return nc


_NC_CACHE = {}


def kernel(L0_attn_hidden_states, L0_attn_Wq, L0_attn_Wk, L0_attn_Wv,
           L0_attn_Wg, L0_attn_Ws, L0_attn_Wo, L0_attn_g_norm_weight,
           L0_ffn_gate_w, L0_ffn_up_w, L0_ffn_down_w, L0_ffn_gate_scale,
           L0_ffn_up_scale, L0_ffn_down_scale):
    x = np.ascontiguousarray(np.asarray(L0_attn_hidden_states, np.float32))
    B, T, _ = x.shape
    shared = {
        "wq": np.ascontiguousarray(L0_attn_Wq, np.float32),
        "wk": np.ascontiguousarray(L0_attn_Wk, np.float32),
        "ws": np.ascontiguousarray(L0_attn_Ws, np.float32),
        "wv": np.ascontiguousarray(L0_attn_Wv, np.float32),
        "wg": np.ascontiguousarray(L0_attn_Wg, np.float32),
        "wo": np.ascontiguousarray(L0_attn_Wo, np.float32),
        "gnw": np.ascontiguousarray(L0_attn_g_norm_weight, np.float32),
        "gatew": np.ascontiguousarray(L0_ffn_gate_w, np.float32),
        "upw": np.ascontiguousarray(L0_ffn_up_w, np.float32),
        "downw": np.ascontiguousarray(L0_ffn_down_w, np.float32),
        "gates": np.ascontiguousarray(L0_ffn_gate_scale, np.float32),
        "ups": np.ascontiguousarray(L0_ffn_up_scale, np.float32),
        "downs": np.ascontiguousarray(L0_ffn_down_scale, np.float32),
    }
    in_maps = []
    for core in range(8):
        b, j = core // 4, core % 4
        pm = np.zeros((128, 4), np.float32)
        pm[:, :j] = 1.0
        m = dict(shared)
        m["x"] = np.ascontiguousarray(x[b, j * TS:(j + 1) * TS, :])
        m["pmask"] = pm
        in_maps.append(m)
    if "nc" not in _NC_CACHE:
        _NC_CACHE["nc"] = build(dbg=False)
    nc = _NC_CACHE["nc"]
    res = run_bass_kernel_spmd(nc, in_maps, core_ids=list(range(8)))
    outs = [np.asarray(res.results[c]["out"]) for c in range(8)]
    full = np.concatenate(outs, axis=0).reshape(B, T, D)
    return full


# version 14
# speedup vs baseline: 1.0208x; 1.0208x over previous
import numpy as np
import concourse.bass as bass
import concourse.mybir as mybir
from concourse.bass_utils import run_bass_kernel_spmd

F32 = mybir.dt.float32
BF16 = mybir.dt.bfloat16
FP8 = mybir.dt.float8e4
AF = mybir.ActivationFunctionType
ALU = mybir.AluOpType
AX = mybir.AxisListType

TS = 1024
NCH = 8
D = 2048
NKT = 16
H = 16
DK = 64
DV = 128
M = 64
FI = 6144
NCT = 48
SCALE = 0.125
EPS = 1e-5
KB = 1024

ENGS = ("sync", "scalar", "vector", "tensor", "gpsimd")


class Prog:

    def __init__(self, nc):
        self.nc = nc
        self.ops = []
        self.cnt = {}
        for e in ENGS:
            self.cnt[e + ".c"] = {"sem": nc.alloc_semaphore(f"cc_{e}"), "v": 0}
        self.keyval = {}
        self._last_gc = 0

    def _cnt(self, cname):
        if cname not in self.cnt:
            self.cnt[cname] = {"sem": self.nc.alloc_semaphore(cname), "v": 0}
        return self.cnt[cname]

    def op(self, eng, fn, waits=(), key=None, dma=False, ch=None):
        if dma:
            assert ch is not None, "DMA ops need a channel"
            cname = f"{eng}.q.{ch}"
        else:
            cname = eng + ".c"
        c = self._cnt(cname)
        wt = [w for w in waits if w is not None]
        if eng == "gpsimd" and not dma:
            if self._last_gc:
                wt.append(("gpsimd.c", self._last_gc))
        c["v"] += 16 if dma else 1
        if eng == "gpsimd" and not dma:
            self._last_gc = c["v"]
        if key is not None:
            assert key not in self.keyval, f"dup key {key}"
            self.keyval[key] = (cname, c["v"])
        for w in wt:
            if isinstance(w, str):
                assert w in self.keyval, f"unknown wait key {w}"
        self.ops.append((eng, cname, fn, tuple(wt), c["v"]))

    def emit(self, blk):
        by_eng = {e: [] for e in ENGS}
        for eng, cname, fn, waits, incv in self.ops:
            by_eng[eng].append((cname, fn, waits, incv))
        kv = self.keyval
        cnt = self.cnt

        def mk(ename):
            items = by_eng[ename]

            def body(eng):
                lw = {cn: 0 for cn in cnt}
                prev = {cn: 0 for cn in cnt}
                own = ename + ".c"
                ownv = 0
                for cname, fn, waits, incv in items:
                    if ename in ("vector", "scalar") and cname == own and ownv:
                        eng.wait_ge(cnt[own]["sem"], ownv)
                    for wk in waits:
                        wcn, wv = kv[wk] if isinstance(wk, str) else wk
                        weng = wcn.split(".")[0]
                        wkind = wcn.split(".")[1]
                        if (weng == ename and wkind == "c"
                                and ename != "gpsimd"):
                            continue
                        if lw[wcn] >= wv:
                            continue
                        eng.wait_ge(cnt[wcn]["sem"], wv)
                        lw[wcn] = wv
                    ins = fn(eng)
                    ins.then_inc(cnt[cname]["sem"], incv - prev[cname])
                    prev[cname] = incv
                    if cname == own:
                        ownv = incv
            return body

        for e in ENGS:
            if by_eng[e]:
                getattr(blk, e)(mk(e))


class VT:

    def __init__(self, arena, rowb, base, rows, cols, dtype):
        self.arena, self.rowb, self.base = arena, rowb, base
        self.rows, self.cols, self.dtype = rows, cols, dtype
        self.sz = mybir.dt.size(dtype)
        assert base % 32 == 0 and base + cols * self.sz <= rowb

    def __getitem__(self, idx):
        rs, cs = idx
        r0, r1, st = rs.indices(self.rows)
        c0, c1, st2 = cs.indices(self.cols)
        assert st == 1 and st2 == 1
        return bass.AP(self.arena, r0 * self.rowb + self.base + c0 * self.sz,
                       [[self.rowb, r1 - r0], [1, (c1 - c0) * self.sz]]
                       ).bitcast(self.dtype)


def A(buf, off, pattern):
    if not isinstance(buf, VT):
        return bass.AP(buf, off, pattern)
    rowlen, nrows = pattern[0]
    r0, col = divmod(off, rowlen)
    assert pattern[-1][0] == 1, "innermost dim must be unit-stride"
    fd = [[s * buf.sz, n] for s, n in pattern[1:-1]]
    fd.append([1, pattern[-1][1] * buf.sz])
    return bass.AP(buf.arena, r0 * buf.rowb + buf.base + col * buf.sz,
                   [[buf.rowb, nrows]] + fd).bitcast(buf.dtype)


def build(dbg=False, rd=False, upto="p8", p5c=NCH, p5g=4, p5s=10, dbgcc=False,
          dsel=None, ot_sites=("wod", "p6x", "p7")):
    if dsel is None:
        dsel = {"p0", "p1", "agg", "p5", "ot", "x2t", "ft"}

    def dg(grp):
        return dbg and grp in dsel
    _ph = ["consts", "p0", "p1", "p2", "p3", "p4", "p5", "p6", "p7", "p8"]
    _lim = _ph.index(upto)

    def _on(ph):
        return _ph.index(ph) <= _lim

    nc = bass.Bass("TRN2", target_bir_lowering=False,
                   detect_race_conditions=rd)

    x_in = nc.dram_tensor("x", [TS, D], F32, kind="ExternalInput").ap()
    wts = {}
    for nm, shp in [("wq", [D, 1024]), ("wk", [D, 1024]), ("ws", [D, 1024]),
                    ("wv", [D, 2048]), ("wg", [D, 2048]), ("wo", [D, 2048]),
                    ("gatew", [D, FI]), ("upw", [D, FI]), ("downw", [FI, D])]:
        wts[nm] = nc.dram_tensor(nm, shp, F32, kind="ExternalInput").ap()
    gnw = nc.dram_tensor("gnw", [DV], F32, kind="ExternalInput").ap()
    gates = nc.dram_tensor("gates", [FI], F32, kind="ExternalInput").ap()
    ups = nc.dram_tensor("ups", [FI], F32, kind="ExternalInput").ap()
    downs = nc.dram_tensor("downs", [D], F32, kind="ExternalInput").ap()
    pmask_d = nc.dram_tensor("pmask", [128, 4], F32, kind="ExternalInput").ap()

    out = nc.dram_tensor("out", [TS, D], F32, kind="ExternalOutput").ap()
    x1s = nc.dram_tensor("x1s", [TS, D], F32,
                         kind="ExternalOutput" if dbg else "Internal").ap()
    down8s = nc.dram_tensor("down8s", [FI, D], FP8).ap()
    AGG_KV = 1536
    AGG_S = 1024
    AGG_N = 128 * AGG_KV + AGG_S
    aggin = nc.dram_tensor("aggin", [AGG_N], F32).ap()
    aggout = nc.dram_tensor("aggout", [4 * AGG_N], F32).ap()

    dbgd = {}
    if dbg:
        for nm, shp, dt in [("d_xlnT", [128, 16384], BF16),
                            ("d_kT", [128, 8192], BF16),
                            ("d_eT", [128, 8192], BF16),
                            ("d_qT", [128, 8192], BF16),
                            ("d_v", [128, 16384], BF16),
                            ("d_silug", [128, 16384], BF16),
                            ("d_agg", [128, AGG_KV], F32),
                            ("d_aggs", [1, AGG_S], F32),
                            ("d_sS", [128, 512], F32),
                            ("d_okn", [128, 512], F32),
                            ("d_pvn", [128, 512], BF16),
                            ("d_of", [128, 512], BF16),
                            ("d_oT", [128, 16384], BF16),
                            ("d_x2T", [128, 16384], BF16),
                            ("d_xl2", [128, 16384], BF16),
                            ("d_kT2", [128, 8192], BF16),
                            ("d_scx", [128, 2048], F32),
                            ("d_scxln", [128, 2048], F32),
                            ("d_stat", [128, 32], F32),
                            ("d_fuseT", [128, 49152], BF16)]:
            dbgd[nm] = nc.dram_tensor(nm, shp, dt, kind="ExternalOutput").ap()

    def sb(name, shape, dtype, off):
        return nc.alloc_sbuf_tensor_at(name, shape, dtype, offset=off)

    R = 16512
    c_triu4 = sb("c_triu4", [128, 512], F32, R + 0)
    c_triu_bf = sb("c_triu_bf", [128, 128], BF16, R + 2048)
    c_id_bf = sb("c_id_bf", [128, 128], BF16, R + 2304)
    c_id_f32 = sb("c_id_f32", [128, 128], F32, R + 2560)
    c_ones_bf = sb("c_ones_bf", [128, 16], BF16, R + 3072)
    c_ones_f32 = sb("c_ones_f32", [1, 128], F32, R + 3104)
    c_ones_bfr = sb("c_ones_bfr", [1, 128], BF16, R + 26752)
    st_s_bf = sb("st_s_bf", [1, AGG_S], BF16, R + 27008)
    c_gs = sb("c_gs", [128, 48], F32, R + 3616)
    c_us = sb("c_us", [128, 48], F32, R + 3808)
    c_gnwb = sb("c_gnwb", [128, 128], F32, R + 4000)
    c_pm = sb("c_pm", [128, 4], F32, R + 4512)
    c_bias = sb("c_bias", [128, 4], F32, R + 4544)
    c_dsb = sb("c_dsb", [128, 2048], F32, R + 4608)
    c_gnwrep = sb("c_gnwrep", [1, 128], F32, R + 12800)
    st_kv = sb("st_kv", [128, 1536], F32, R + 13312)
    st_kv_bf = sb("st_kv_bf", [128, 1536], BF16, R + 19456)
    st_s = sb("st_s", [1, AGG_S], F32, R + 22528)
    sc_stat = sb("sc_stat", [128, 32], F32, R + 26624)

    SA = R + 29056
    BBROW = 147456
    sbA = sb("arenaA", [128, 25088], mybir.dt.uint8, SA)
    sbB = sb("arenaB", [128, BBROW], mybir.dt.uint8, SA + 25088)

    def vA(base, rows, cols, dtype):
        return VT(sbA, 25088, base, rows, cols, dtype)

    def vB(base, rows, cols, dtype):
        return VT(sbB, BBROW, base, rows, cols, dtype)

    sc_e = vA(0, 128, 1024, BF16)
    sc_k = vA(2048, 128, 1024, BF16)
    sc_at = vA(4096, 128, 1024, BF16)
    sc_wt = vA(6144, 128, 1024, BF16)
    sc_pvt = vA(8192, 128, 512, BF16)
    sc_sS = vA(9216, 128, 512, F32)
    sc_sinv = vA(11264, 128, 512, F32)
    sc_okn = vA(13312, 128, 512, F32)
    sc_pv = vA(15360, 128, 512, F32)
    sc_pvn = vA(17408, 128, 512, BF16)
    sc_osq = vA(18432, 128, 512, F32)
    sc_of = vA(20480, 128, 512, BF16)
    sc_h1 = vA(21504, 128, 16, F32)
    sc_h2 = vA(21568, 128, 16, F32)
    sc_h3 = vA(21632, 128, 16, F32)
    sc_h4 = vA(21696, 128, 16, F32)
    sc_h5 = vA(21760, 128, 16, F32)
    sc_h6 = vA(21824, 128, 16, F32)
    sc_x = [vA(i * 8192, 128, 2048, F32) for i in range(2)]
    sc_xln = vA(16384, 128, 2048, F32)
    c_dsrep = vA(16384, 1, 2048, F32)
    ring_f = [vA(i * 2048, 128, 512, F32) for i in range(2)]
    ring_b = [vA(4096 + i * 1024, 128, 512, BF16) for i in range(4)]
    sw_b = [vA(8192 + i * 1024, 128, 512, BF16) for i in range(2)]
    sw_f = [vA(10240 + i * 2048, 128, 512, F32) for i in range(2)]
    pf_kv = vA(14848, 128, AGG_KV, F32)
    c_gnwb4 = vA(14848, 128, 512, F32)
    pf_s = vA(20992, 1, AGG_S, F32)
    r8 = [vA(i * 2048, 128, 2048, FP8) for i in range(4)]
    sg_sc = vA(8192, 128, 512, F32)
    p8_x1 = [vA(10240 + i * 2048, 128, 512, F32) for i in range(2)]
    p8_o = [vA(14336 + i * 2048, 128, 512, F32) for i in range(2)]

    xlnT = vB(0, 128, 16384, BF16)
    kT = vB(32 * KB, 128, 8192, BF16)
    eT = vB(48 * KB, 128, 8192, BF16)
    v_nat = vB(64 * KB, 128, 16384, BF16)
    qT = vB(96 * KB, 128, 8192, BF16)
    silug = vB(112 * KB, 128, 16384, BF16)
    oT = vB(0, 128, 16384, BF16)
    p6_x = [vB(32 * KB + i * 4096, 128, 1024, F32) for i in range(4)]
    p6_x1 = [vB(48 * KB + i * 8192, 128, 2048, F32) for i in range(4)]
    wo_ring = [vB(80 * KB + i * 2048, 128, 1024, BF16) for i in range(4)]
    x2T = vB(96 * KB, 128, 16384, BF16)
    fuseT = vB(0, 128, 49152, BF16)
    xln2a = vB(32 * KB, 128, 2048, F32)
    xln2b = vB(88 * KB, 128, 2048, F32)
    down8 = vB(96 * KB, 128, 24576, FP8)

    PB = [nc.alloc_psum_tensor(f"pb{i}", [128, 512], F32) for i in range(8)]
    BREL = {b: None for b in range(8)}
    SREL = {("b", 0): None, ("b", 1): None, ("f", 0): None, ("f", 1): None}

    p = Prog(nc)
    O = p.op

    def _noop(eng, fn, waits=(), key=None, dma=False, ch=None):
        if key is not None:
            p.keyval[key] = ("tensor.c", 0)

    def _gate(ph):
        return p.op if _on(ph) else _noop


    O("sync", lambda e: e.dma_start(
        out=c_gs[:, :], in_=A(gates.tensor, 0, [[1, 128], [128, 48]]),
        allow_slow_non_contiguous=True),
      key="ld_gs", dma=True, ch="cst")
    O("sync", lambda e: e.dma_start(
        out=c_us[:, :], in_=A(ups.tensor, 0, [[1, 128], [128, 48]]),
        allow_slow_non_contiguous=True),
      waits=("ld_gs",), key="ld_us", dma=True, ch="cst")
    O("sync", lambda e: e.dma_start(
        out=c_gnwrep[0:1, :],
        in_=A(gnw.tensor, 0, [[128, 1], [1, 128]])),
      waits=("ld_us",), key="ld_gnw", dma=True, ch="cst")
    O("sync", lambda e: e.dma_start(out=c_pm[:, :], in_=pmask_d),
      waits=("ld_gnw",), key="ld_pm", dma=True, ch="cst")

    O("gpsimd", lambda e: e.memset(c_bias[:, 0:1], -32.0))
    O("gpsimd", lambda e: e.memset(c_bias[:, 1:2], float(EPS)), key="bias_c")
    O("gpsimd", lambda e: e.memset(c_triu4[:, :], 1.0))
    O("gpsimd", lambda e: e.affine_select(
        out=c_triu4[:, :].rearrange("q (r f) -> q r f", f=128),
        in_=c_triu4[:, :].rearrange("q (r f) -> q r f", f=128),
        pattern=[[0, 4], [1, 128]], compare_op=ALU.is_ge, fill=0.0,
        base=0, channel_multiplier=-1))
    O("gpsimd", lambda e: e.memset(c_triu_bf[:, :], 1.0))
    O("gpsimd", lambda e: e.affine_select(
        out=c_triu_bf[:, :], in_=c_triu_bf[:, :], pattern=[[1, 128]],
        compare_op=ALU.is_ge, fill=0.0, base=0, channel_multiplier=-1))
    O("gpsimd", lambda e: e.memset(c_id_bf[:, :], 1.0))
    O("gpsimd", lambda e: e.affine_select(
        out=c_id_bf[:, :], in_=c_id_bf[:, :], pattern=[[1, 128]],
        compare_op=ALU.is_equal, fill=0.0, base=0, channel_multiplier=-1))
    O("gpsimd", lambda e: e.memset(c_id_f32[:, :], 1.0))
    O("gpsimd", lambda e: e.affine_select(
        out=c_id_f32[:, :], in_=c_id_f32[:, :], pattern=[[1, 128]],
        compare_op=ALU.is_equal, fill=0.0, base=0, channel_multiplier=-1))
    O("gpsimd", lambda e: e.memset(c_ones_bf[:, :], 1.0))
    O("gpsimd", lambda e: e.memset(c_ones_f32[:, :], 1.0))
    O("gpsimd", lambda e: e.memset(st_kv[:, :], 0.0))
    O("gpsimd", lambda e: e.memset(st_kv_bf[:, :], 0.0))
    O("gpsimd", lambda e: e.memset(c_ones_bfr[:, :], 1.0))
    O("gpsimd", lambda e: e.memset(st_s[:, :], 0.0))
    O("gpsimd", lambda e: e.memset(sc_stat[:, :], 0.0))
    O("gpsimd", lambda e: e.memset(st_s_bf[:, :], 0.0), key="consts_g")
    O("tensor", lambda e: e.matmul(PB[0][:, 0:128], c_ones_f32[0:1, :],
                                   c_gnwrep[0:1, :], start=True, stop=True),
      waits=("consts_g", "ld_gnw"), key="mm_gnwb")
    O("vector", lambda e: e.tensor_copy(c_gnwb[:, :], PB[0][:, 0:128]),
      waits=("mm_gnwb",), key="ev_gnwb")
    BREL[0] = "ev_gnwb"
    def emit_ln_chunk(c, tag, src_key, src_ap, dstT, dstT_cols, prev_gate,
                      xbuf=None):
        sb8 = 8 * (c % 2)
        st = sc_stat[:, sb8:sb8 + 8]
        xb = xbuf if xbuf is not None else sc_xln
        O("vector", lambda e: e.reduce_sum(st[:, 0:1], src_ap, axis=AX.X),
          waits=(src_key,), key=f"{tag}rs{c}")
        O("scalar", lambda e: e.activation(xb[:, :], src_ap, AF.Square,
                                           accum_out=st[:, 1:2]),
          waits=(src_key, prev_gate), key=f"{tag}sq{c}")
        O("vector", lambda e: e.tensor_scalar(
            st[:, 0:1], st[:, 0:1], 1.0 / D, None, ALU.mult),
          key=f"{tag}mn{c}")
        O("vector", lambda e: e.tensor_tensor(
            st[:, 2:3], st[:, 0:1], st[:, 0:1], ALU.mult))
        O("vector", lambda e: e.scalar_tensor_tensor(
            st[:, 3:4], st[:, 1:2], 1.0 / D, st[:, 2:3],
            ALU.mult, ALU.subtract),
          waits=(f"{tag}sq{c}",), key=f"{tag}var{c}")
        O("scalar", lambda e: e.activation(st[:, 4:5], st[:, 3:4], AF.Ln,
                                           bias=c_bias[:, 1:2]),
          waits=(f"{tag}var{c}", "bias_c"))
        O("scalar", lambda e: e.activation(st[:, 5:6], st[:, 4:5], AF.Exp,
                                           scale=-0.5), key=f"{tag}rst{c}")
        O("vector", lambda e: e.tensor_scalar(
            xb[:, :], src_ap, st[:, 0:1], st[:, 5:6],
            ALU.subtract, ALU.mult),
          waits=(f"{tag}rst{c}", prev_gate), key=f"{tag}ap{c}")
        evks = []
        for ktg in range(4):
            b = ktg
            for j in range(4):
                kt = ktg * 4 + j
                wl = [f"{tag}ap{c}"] if j == 0 else []
                if j == 0 and BREL[b]:
                    wl.append(BREL[b])
                O("tensor", lambda e, kt=kt, b=b, j=j, xb=xb: e.transpose(
                    PB[b][:, j * 128:(j + 1) * 128],
                    xb[:, kt * 128:(kt + 1) * 128], c_id_f32[:, :]),
                  waits=wl, key=f"{tag}tp{c}_{ktg}" if j == 3 else None)
            ek = f"{tag}ev{c}_{ktg}"
            for j in range(4):
                O("vector", lambda e, ktg=ktg, b=b, j=j: e.tensor_copy(
                    A(dstT, (ktg * 4 + j) * dstT_cols + c * 128,
                      [[16384, 128], [1, 128]]),
                    PB[b][:, j * 128:(j + 1) * 128]),
                  waits=(f"{tag}tp{c}_{ktg}",) if j == 0 else (),
                  key=ek if j == 3 else None)
            BREL[b] = ek
            evks.append(ek)
        return evks

    O = _gate("p0")
    xl_keys = {}
    for c in range(NCH):
        O("sync", lambda e, c=c: e.dma_start(
            out=sc_x[c % 2][:, :], in_=x_in[c * 128:(c + 1) * 128, :]),
          waits=((f"P0ap{c-2}",) if c >= 2 else ()), key=f"x0_{c}",
          dma=True, ch=f"x{c % 2}")
        prev_gate = f"P0ev{c-2}_3" if c >= 2 else "ev_gnwb"
        evks = emit_ln_chunk(c, "P0", f"x0_{c}", sc_x[c % 2][:, :],
                             xlnT, 1024, prev_gate,
                             xbuf=(sc_xln if c % 2 == 0 else xln2a))
        for ktg in range(4):
            xl_keys[(c, ktg)] = evks[ktg]

    if dg("p0"):
        O("sync", lambda e: e.dma_start(out=dbgd["d_xl2"], in_=xlnT[:, :]),
          waits=(xl_keys[(NCH - 1, 3)],), dma=True, ch="dbg3", key="dmp_xl2")
        O("sync", lambda e: e.dma_start(out=dbgd["d_scx"], in_=sc_x[1][:, :]),
          waits=("dmp_xl2",), dma=True, ch="dbg3", key="dmp_scx")
        O("sync", lambda e: e.dma_start(out=dbgd["d_scxln"], in_=sc_xln[:, :]),
          waits=("dmp_scx",), dma=True, ch="dbg3", key="dmp_scxln")
        O("sync", lambda e: e.dma_start(out=dbgd["d_stat"], in_=sc_stat[:, :]),
          waits=("dmp_scxln",), dma=True, ch="dbg3", key="dmp_stat")

    sw = {"i": 0}

    def emit_sweep(w_ap, ncols, tag, evac, first_wait=None, tout=False):
        nng = ncols // 512
        j0 = sw["i"]
        for ng in range(nng):
            mmlast = None
            for kt in range(NKT):
                i = sw["i"]
                sw["i"] += 1
                qeng = "sync" if (i % 2 == 0) else "scalar"
                wld = [f"swcv_{i-2}"] if i >= 2 else []
                if first_wait and i - j0 < 2:
                    wld.append(first_wait)
                O(qeng, lambda e, kt=kt, ng=ng, i=i, w=w_ap: e.dma_start(
                    out=ring_f[i % 2][:, :],
                    in_=w[kt * 128:(kt + 1) * 128, ng * 512:(ng + 1) * 512]),
                  waits=wld, key=f"swd_{i}", dma=True, ch="sw")
                O("vector", lambda e, i=i: e.tensor_copy(
                    ring_b[i % 4][:, :], ring_f[i % 2][:, :]),
                  waits=(f"swd_{i}",) + ((f"swmm_{i-4}",) if i >= 4 else ()),
                  key=f"swcv_{i}")
                for s8i in range(NCH):
                    wl = [f"swcv_{i}"]
                    if kt == 0:
                        wl.append(BREL[s8i])
                    last = (s8i == NCH - 1)
                    if tout:
                        p2, th = s8i // 2, s8i % 2
                        wl.append(xl_keys[(4 * th + 3, kt // 4)])
                        O("tensor", lambda e, p2=p2, th=th, kt=kt, i=i,
                          b=s8i: e.matmul(
                            PB[b][:, :],
                            bass.AP(ring_b[i % 4].arena,
                                    ring_b[i % 4].base + p2 * 256,
                                    [[25088, 128], [1, 256]]).bitcast(BF16),
                            A(xlnT, kt * 1024 + th * 512,
                              [[16384, 128], [1, 512]]),
                            start=(kt == 0), stop=(kt == NKT - 1)),
                          waits=wl, key=f"swmm_{i}" if last else None)
                    else:
                        wl.append(xl_keys[(s8i, kt // 4)])
                        O("tensor", lambda e, c=s8i, kt=kt, i=i: e.matmul(
                            PB[c][:, :],
                            A(xlnT, kt * 1024 + c * 128,
                                    [[16384, 128], [1, 128]]),
                            ring_b[i % 4][:, :],
                            start=(kt == 0), stop=(kt == NKT - 1)),
                          waits=wl, key=f"swmm_{i}" if last else None)
                    if last:
                        mmlast = f"swmm_{i}"
            for s8i in range(NCH):
                BREL[s8i] = evac(s8i, ng, mmlast)

    def ev_transposed(dstT, tag):
        def evac(slot, ng, mmkey):
            p2, th = slot // 2, slot % 2
            pr = ng * 4 + p2
            O("vector", lambda e, pr=pr, th=th, slot=slot: e.tensor_copy(
                A(dstT, pr * 1024 + th * 512, [[8192, 128], [1, 512]]),
                PB[slot][:, :]),
              waits=(mmkey,), key=f"{tag}_{pr}_{th}")
            return f"{tag}_{pr}_{th}"
        return evac

    def ev_s(slot, ng, mmkey):
        p2, th = slot // 2, slot % 2
        pr = ng * 4 + p2
        fb = sw_f[slot % 2]
        O("vector", lambda e, slot=slot, fb=fb: e.tensor_scalar(
            fb[:, :], PB[slot][:, :], -32.0, 32.0, ALU.max, ALU.min),
          waits=(mmkey, SREL[("f", slot % 2)]),
          key=f"Pscl_{pr}_{th}")
        SREL[("f", slot % 2)] = f"Pse_{pr}_{th}"
        O("scalar", lambda e, pr=pr, th=th, fb=fb: e.activation(
            A(eT, pr * 1024 + th * 512, [[8192, 128], [1, 512]]),
            fb[:, :], AF.Exp, bias=c_bias[:, 0:1]),
          waits=(f"Pscl_{pr}_{th}", "bias_c"), key=f"Pse_{pr}_{th}")
        return f"Pse_{pr}_{th}"

    def ev_plain(dst, tag):
        def evac(c, ng, mmkey):
            O("vector", lambda e, c=c, ng=ng: e.tensor_copy(
                A(dst, c * 2048 + ng * 512, [[16384, 128], [1, 512]]),
                PB[c][:, :]),
              waits=(mmkey,), key=f"{tag}ev{c}_{ng}")
            return f"{tag}ev{c}_{ng}"
        return evac

    def ev_g(c, ng, mmkey):
        O("scalar", lambda e, c=c: e.activation(
            sg_sc[:, :], PB[c][:, :], AF.Silu),
          waits=(mmkey,) + ((f"Pgev{c-1}_{ng}" if c >= 1 else
                             (f"Pgev{NCH-1}_{ng-1}" if ng >= 1 else None),)),
          key=f"Pgsl{c}_{ng}")
        O("vector", lambda e, c=c, ng=ng: e.tensor_tensor(
            A(silug, c * 2048 + ng * 512, [[16384, 128], [1, 512]]),
            sg_sc[:, :], c_gnwb4[:, :], ALU.mult),
          waits=(f"Pgsl{c}_{ng}", "gnwb4"), key=f"Pgev{c}_{ng}")
        return f"Pgev{c}_{ng}"

    O = _gate("p1")
    emit_sweep(wts["wk"], 1024, "Pk", ev_transposed(kT, "Pkt"),
               first_wait="dmp_xl2" if dg("p0") else None, tout=True)
    if dg("p0"):
        O("sync", lambda e: e.dma_start(out=dbgd["d_kT2"], in_=kT[:, :]),
          waits=("Pkt_7_1",), dma=True, ch="dbg3", key="dmp_kT2")
    emit_sweep(wts["ws"], 1024, "Ps", ev_s,
               first_wait="dmp_kT2" if dg("p0") else None, tout=True)
    emit_sweep(wts["wv"], 2048, "Pv", ev_plain(v_nat, "Pv"))

    O = _gate("p2")
    def emit_ek_derive(c, tag, bnk_e, bnk_k, wait_extra=()):
        assert bnk_e != bnk_k
        keys = {}
        for half in range(2):
            for j in range(4):
                pr = half * 4 + j
                wl = list(wait_extra)
                wl.append(f"Pse_{pr}_{c // 4}")
                if j == 0:
                    wl.append(BREL[bnk_e])
                O("tensor", lambda e, pr=pr, j=j, c=c: e.transpose(
                    PB[bnk_e][:, :].bitcast(BF16)[:, j * 128:(j + 1) * 128],
                    A(eT, pr * 1024 + c * 128, [[8192, 128], [1, 128]]),
                    c_id_bf[:, :]),
                  waits=wl, key=f"{tag}et{c}_{half}" if j == 3 else None)
                wl2 = [f"Pkt_{pr}_{c // 4}"]
                if j == 0:
                    wl2.append(BREL[bnk_k])
                O("tensor", lambda e, pr=pr, j=j, c=c: e.transpose(
                    PB[bnk_k][:, :].bitcast(BF16)[:, j * 128:(j + 1) * 128],
                    A(kT, pr * 1024 + c * 128, [[8192, 128], [1, 128]]),
                    c_id_bf[:, :]),
                  waits=wl2, key=f"{tag}kt{c}_{half}" if j == 3 else None)
            ke = f"{tag}ee{c}_{half}"
            O("vector", lambda e, half=half, c=c: e.tensor_copy(
                sc_e[:, half * 512:(half + 1) * 512],
                PB[bnk_e][:, :].bitcast(BF16)[:, 0:512]),
              waits=(f"{tag}et{c}_{half}",), key=ke)
            BREL[bnk_e] = ke
            kk = f"{tag}ke{c}_{half}"
            O("vector", lambda e, half=half, c=c: e.tensor_copy(
                sc_k[:, half * 512:(half + 1) * 512],
                PB[bnk_k][:, :].bitcast(BF16)[:, 0:512]),
              waits=(f"{tag}kt{c}_{half}",), key=kk)
            BREL[bnk_k] = kk
            keys[half] = (ke, kk)
        return keys

    O("gpsimd", lambda e: e.memset(pf_s[:, :], 0.0),
      waits=(f"P0tp{NCH-1}_3",), key="pfs0")
    O("gpsimd", lambda e: e.memset(pf_kv[:, :], 0.0),
      waits=("pfs0",), key="pfkv0")
    for c in range(NCH):
        dk = emit_ek_derive(c, f"A{c}", 3, 4)
        for h in range(H):
            pr, par = h // 2, h % 2
            tp = (0, 64) if par else None
            half = h // 8
            wl = [dk[half][0], dk[half][1]]
            if h == 0:
                wl += [BREL[0], BREL[1], BREL[2], BREL[6], BREL[7]]
            O("tensor", lambda e, h=h, pr=pr, par=par, tp=tp: e.matmul(
                PB[0][par * 64:par * 64 + 64, pr * 64:(pr + 1) * 64],
                sc_k[:, h * 64:(h + 1) * 64], sc_e[:, h * 64:(h + 1) * 64],
                start=True, stop=True, tile_position=tp),
              waits=wl, key=f"Ahk{c}_{h}")
            bv = 1 + pr // 4
            O("tensor", lambda e, h=h, pr=pr, par=par, tp=tp, c=c, bv=bv:
              e.matmul(
                PB[bv][par * 64:par * 64 + 64,
                       (pr % 4) * 128:(pr % 4 + 1) * 128],
                sc_e[:, h * 64:(h + 1) * 64],
                A(v_nat, c * 2048 + h * 128, [[16384, 128], [1, 128]]),
                start=True, stop=True, tile_position=tp),
              waits=(f"Pvev{c}_{h // 4}",), key=f"Ahv{c}_{h}")
            bs = 6 + h // 8
            O("tensor", lambda e, h=h, bs=bs: e.matmul(
                PB[bs][0:1, (h % 8) * 64:(h % 8 + 1) * 64],
                c_ones_bf[:, 0:1], sc_e[:, h * 64:(h + 1) * 64],
                start=True, stop=True),
              key=f"Ahs{c}_{h}")
        for pr in range(8):
            O("vector", lambda e, pr=pr: e.tensor_tensor(
                A(pf_kv, pr * 192, [[1536, 128], [1, 64]]),
                A(pf_kv, pr * 192, [[1536, 128], [1, 64]]),
                PB[0][:, pr * 64:(pr + 1) * 64], ALU.add),
              waits=(f"Ahk{c}_{H-1}", "pfkv0") if pr == 0 else (),
              key=f"Aak{c}" if pr == 7 else None)
        BREL[0] = f"Aak{c}"
        for b in (1, 2):
            for j in range(4):
                pr = (b - 1) * 4 + j
                O("vector", lambda e, b=b, j=j, pr=pr: e.tensor_tensor(
                    A(pf_kv, 64 + pr * 192, [[1536, 128], [1, 128]]),
                    A(pf_kv, 64 + pr * 192, [[1536, 128], [1, 128]]),
                    PB[b][:, j * 128:(j + 1) * 128], ALU.add),
                  waits=(f"Ahv{c}_{H-1}",) if j == 0 else (),
                  key=f"Aav{c}_{b}" if j == 3 else None)
            BREL[b] = f"Aav{c}_{b}"
        for b in (6, 7):
            O("vector", lambda e, b=b: e.tensor_tensor(
                pf_s[0:1, (b - 6) * 512:(b - 5) * 512],
                pf_s[0:1, (b - 6) * 512:(b - 5) * 512],
                PB[b][0:1, :], ALU.add),
              waits=(f"Ahs{c}_{H-1}", "pfs0"), key=f"Aas{c}_{b}")
            BREL[b] = f"Aas{c}_{b}"
    p.keyval["Aevk"] = p.keyval[f"Aak{NCH-1}"]
    p.keyval["Aevv1"] = p.keyval[f"Aav{NCH-1}_1"]
    p.keyval["Aevv2"] = p.keyval[f"Aav{NCH-1}_2"]
    p.keyval["Aevs6"] = p.keyval[f"Aas{NCH-1}_6"]
    p.keyval["Aevs7"] = p.keyval[f"Aas{NCH-1}_7"]
    O("sync", lambda e: e.dma_start(
        out=A(aggin.tensor, 0, [[1536, 128], [1, 1536]]),
        in_=pf_kv[:, :]), waits=("Aevk", "Aevv1", "Aevv2"),
      key="Adkv", dma=True, ch="agg")
    O("sync", lambda e: e.dma_start(
        out=A(aggin.tensor, 128 * 1536, [[1024, 1], [1, 1024]]),
        in_=pf_s[0:1, :]), waits=("Aevs6", "Aevs7", "Adkv"),
      key="Ads", dma=True, ch="agg")
    if dg("agg"):
        O("sync", lambda e: e.dma_start(out=dbgd["d_agg"], in_=pf_kv[:, :]),
          waits=("Adkv",), dma=True, ch="dbg2", key="dmp_d_agg")
        O("sync", lambda e: e.dma_start(out=dbgd["d_aggs"], in_=pf_s[0:1, :]),
          waits=("Ads",), dma=True, ch="dbg2", key="dmp_d_aggs")
    if (not dbg) or dbgcc:
        O("gpsimd", lambda e: e.collective_compute(
            "AllGather", ALU.bypass,
            replica_groups=[[0, 1, 2, 3], [4, 5, 6, 7]],
            ins=[aggin.opt()], outs=[aggout.opt()]),
          waits=("Adkv", "Ads"), key="cc")

    O = _gate("p3")
    for j in range(4):
        O("vector", lambda e, j=j: e.tensor_copy(
            c_gnwb4[:, j * 128:(j + 1) * 128], c_gnwb[:, :]),
          waits=(("Adkv", "ev_gnwb")
                 + (("dmp_d_agg", "dmp_d_aggs") if dg("agg") else ()))
          if j == 0 else (),
          key="gnwb4" if j == 3 else None)
    emit_sweep(wts["wq"], 1024, "Pq", ev_transposed(qT, "Pqt"),
               first_wait=f"Ahs{NCH-1}_{H-1}", tout=True)
    emit_sweep(wts["wg"], 2048, "Pg", ev_g)

    O = _gate("p4")
    if (not dbg) or dbgcc:
        for r in range(4):
            O("gpsimd", lambda e, r=r: e.dma_start(
                out=pf_kv[:, :],
                in_=A(aggout.tensor, r * AGG_N,
                            [[1536, 128], [1, 1536]])),
              waits=("cc",) + ((f"P4kv{r-1}",) if r else
                               ("Adkv", f"Pgev{NCH-1}_3")),
              key=f"P4dkv{r}", dma=True, ch="p4a")
            O("vector", lambda e, r=r: e.scalar_tensor_tensor(
                st_kv[:, :], pf_kv[:, :], c_pm[:, r:r + 1], st_kv[:, :],
                ALU.mult, ALU.add),
              waits=(f"P4dkv{r}", "ld_pm"), key=f"P4kv{r}")
            O("gpsimd", lambda e, r=r: e.dma_start(
                out=pf_s[0:1, :],
                in_=A(aggout.tensor, r * AGG_N + 128 * 1536,
                            [[1024, 1], [1, 1024]])),
              waits=("cc",) + ((f"P4s{r-1}",) if r else ("Ads",)),
              key=f"P4ds{r}", dma=True, ch="p4b")
            O("vector", lambda e, r=r: e.scalar_tensor_tensor(
                st_s[0:1, :], pf_s[0:1, :], c_pm[0:1, r:r + 1],
                st_s[0:1, :], ALU.mult, ALU.add),
              waits=(f"P4ds{r}",), key=f"P4s{r}")
        O("vector", lambda e: e.tensor_copy(st_kv_bf[:, :], st_kv[:, :]),
          waits=("P4kv3",), key="P4bf")
        O("vector", lambda e: e.tensor_copy(st_s_bf[0:1, :], st_s[0:1, :]),
          waits=("P4s3",), key="P4sb")
        st_ready = ["P4bf", "P4sb"]
    else:
        st_ready = ["consts_g"]

    _d8_full = _on("p5") and p5c >= 3 and p5g >= 2 and p5s >= 10

    def emit_d8(w0):
        for ct in range(NCT):
            O("gpsimd", lambda e, ct=ct: e.dma_start(
                out=down8s[ct * 128:(ct + 1) * 128, :],
                in_=wts["downw"][ct * 128:(ct + 1) * 128, :]),
              waits=w0 if ct == 0 else (),
              key=f"d8_{ct}", dma=True, ch="d8")

    if not _d8_full:
        emit_d8((f"Pgev{NCH-1}_3",) if _on("p3") else ())

    if dg("p1"):
        for nm, buf, wk in [("d_xlnT", xlnT, xl_keys[(NCH - 1, 3)]),
                            ("d_kT", kT, "Pkt_7_1"),
                            ("d_eT", eT, "Pse_7_1"),
                            ("d_qT", qT, "Pqt_7_1"),
                            ("d_v", v_nat, f"Pvev{NCH-1}_3"),
                            ("d_silug", silug, f"Pgev{NCH-1}_3")]:
            O("sync", lambda e, nm=nm, buf=buf: e.dma_start(
                out=dbgd[nm], in_=buf[:, :]), waits=(wk,), dma=True,
              ch="dbg2", key=f"dmp_{nm}")

    O = _gate("p5")
    stkey = {("kv", pr): st_ready[0] for pr in range(8)}
    for pr in range(8):
        stkey[("s", pr)] = st_ready[-1]
    last_snew = [None]

    for c in range(min(NCH, p5c)):
        O = _gate("p5")
        if c == 2 and _d8_full:
            emit_d8(("c1G1sbf",))
        dk = emit_ek_derive(c, f"E{c}", 3, 4)
        ek_e = [dk[0][0], dk[1][0]]
        ek_k = [dk[0][1], dk[1][1]]
        dbg_gate = []
        for G in range(min(2, p5g)):
            hs = [8 * G + i for i in range(8)]
            prs = [4 * G + j for j in range(4)]
            tg = f"c{c}G{G}"
            O = p.op if (_lim >= 9 or p5s >= 1) and _on("p5") else _noop
            for i, h in enumerate(hs):
                pr, par = h // 2, h % 2
                bk = 0 if par == 0 else 4
                wl = [f"Pkt_{pr}_{c // 4}", f"Pqt_{pr}_{c // 4}"]
                if i == 0:
                    wl += [BREL[0], BREL[4]]
                O("tensor", lambda e, pr=pr, par=par, i=i, c=c, bk=bk:
                  e.matmul(
                    PB[bk][:, (i // 2) * 128:(i // 2 + 1) * 128],
                    A(kT, pr * 1024 + c * 128 + par * 64 * 8192,
                      [[8192, 64], [1, 128]]),
                    A(qT, pr * 1024 + c * 128 + par * 64 * 8192,
                      [[8192, 64], [1, 128]]),
                    start=True, stop=True),
                  waits=wl, key=f"{tg}at" if i == 7 else None)
            O = p.op if (_lim >= 9 or p5s >= 1.5) and _on("p5") else _noop
            for i in range(8):
                bk = 0 if i % 2 == 0 else 4
                O("vector", lambda e, i=i, bk=bk: e.tensor_tensor(
                    A(sc_at, i * 128, [[1024, 128], [1, 128]]),
                    PB[bk][:, (i // 2) * 128:(i // 2 + 1) * 128],
                    c_triu4[:, 0:128], ALU.mult),
                  waits=(f"{tg}at",) if i == 0 else (),
                  key=f"{tg}atm" if i == 7 else None)
            BREL[0] = f"{tg}atm"
            BREL[4] = f"{tg}atm"
            O = p.op if (_lim >= 9 or p5s >= 2) and _on("p5") else _noop
            for i, h in enumerate(hs):
                wl = [ek_e[h // 8], stkey[("s", h // 2)]]
                if i == 0:
                    wl.append(BREL[1])
                O("tensor", lambda e, h=h, i=i: e.matmul(
                    PB[1][:, i * 64:(i + 1) * 64], c_triu_bf[:, :],
                    sc_e[:, h * 64:(h + 1) * 64], start=True, stop=False),
                  waits=wl)
                O("tensor", lambda e, h=h, i=i: e.matmul(
                    PB[1][:, i * 64:(i + 1) * 64], c_ones_bfr[0:1, :],
                    st_s_bf[0:1, h * 64:(h + 1) * 64], start=False,
                    stop=True),
                  key=f"{tg}cs" if i == 7 else None)
            O("vector", lambda e: e.tensor_copy(sc_sS[:, :], PB[1][:, :]),
              waits=(f"{tg}cs", last_snew[0]) + tuple(dbg_gate),
              key=f"{tg}sev")
            BREL[1] = f"{tg}sev"
            O("vector", lambda e: e.reciprocal(sc_sinv[:, :], sc_sS[:, :]),
              key=f"{tg}sinv")
            O("gpsimd", lambda e, G=G: e.dma_start(
                out=st_s_bf[0:1, G * 512:(G + 1) * 512],
                in_=sc_sS[127:128, :]),
              waits=(f"{tg}sev",), key=f"{tg}snew", dma=True, ch=f"sn{G}")
            last_snew[0] = f"{tg}snew"
            for pr in prs:
                stkey[("s", pr)] = f"{tg}snew"
            O = p.op if (_lim >= 9 or p5s >= 3) and _on("p5") else _noop
            for i, h in enumerate(hs):
                pr, par = h // 2, h % 2
                wl = [f"{tg}atm"]
                if i == 0:
                    wl.append(BREL[2])
                    wl.append(stkey[("kv", prs[0])])
                O("tensor", lambda e, h=h, i=i: e.matmul(
                    PB[2][:, i * 64:(i + 1) * 64],
                    A(sc_at, i * 128, [[1024, 128], [1, 128]]),
                    sc_e[:, h * 64:(h + 1) * 64], start=True, stop=False),
                  waits=wl)
                O("tensor", lambda e, h=h, pr=pr, par=par, i=i, c=c: e.matmul(
                    PB[2][:, i * 64:(i + 1) * 64],
                    A(qT, pr * 1024 + c * 128 + par * 64 * 8192,
                      [[8192, 64], [1, 128]]),
                    A(st_kv_bf, pr * 192 + par * 64 * 1536,
                      [[1536, 64], [1, 64]]),
                    start=False, stop=True),
                  key=f"{tg}ok" if i == 7 else None)
            O("vector", lambda e: e.scalar_tensor_tensor(
                sc_okn[:, :], PB[2][:, :], SCALE, sc_sinv[:, :],
                ALU.mult, ALU.mult),
              waits=(f"{tg}ok", f"{tg}sinv"), key=f"{tg}okn")
            BREL[2] = f"{tg}okn"
            O = p.op if (_lim >= 9 or p5s >= 4) and _on("p5") else _noop
            O("vector", lambda e: e.reduce_max(
                sc_h1[:, 0:8],
                sc_okn[:, :].rearrange("q (r f) -> q r f", f=64), axis=AX.X),
              key=f"{tg}mx")
            O("vector", lambda e: e.tensor_scalar(
                sc_h2[:, 0:8], sc_h1[:, 0:8], -1.0, None, ALU.mult),
              key=f"{tg}nm")
            for i in range(8):
                O("scalar", lambda e, i=i: e.activation(
                    sc_pv[:, i * 64:(i + 1) * 64],
                    sc_okn[:, i * 64:(i + 1) * 64], AF.Exp,
                    bias=sc_h2[:, i:i + 1],
                    accum_out=sc_h3[:, i:i + 1]),
                  waits=(f"{tg}nm",), key=f"{tg}ex{i}")
            O("vector", lambda e: e.reciprocal(sc_h4[:, 0:8], sc_h3[:, 0:8]),
              waits=(f"{tg}ex7",), key=f"{tg}rs")
            for i in range(8):
                O("vector", lambda e, i=i: e.scalar_tensor_tensor(
                    sc_pvn[:, i * 64:(i + 1) * 64],
                    sc_pv[:, i * 64:(i + 1) * 64], sc_h4[:, i:i + 1],
                    sc_sinv[:, i * 64:(i + 1) * 64], ALU.mult, ALU.mult),
                  waits=(f"{tg}rs",), key=f"{tg}pvn" if i == 7 else None)
            O = p.op if (_lim >= 9 or p5s >= 5) and _on("p5") else _noop
            for j in range(4):
                wl = [f"{tg}pvn"]
                if j == 0:
                    wl.append(BREL[3])
                O("tensor", lambda e, j=j: e.transpose(
                    PB[3][:, :].bitcast(BF16)[:, j * 128:(j + 1) * 128],
                    sc_pvn[:, j * 128:(j + 1) * 128], c_id_bf[:, :]),
                  waits=wl, key=f"{tg}pt" if j == 3 else None)
            O("vector", lambda e: e.tensor_copy(
                sc_pvt[:, :], PB[3][:, :].bitcast(BF16)[:, 0:512]),
              waits=(f"{tg}pt",), key=f"{tg}pte")
            BREL[3] = f"{tg}pte"
            O = p.op if (_lim >= 9 or p5s >= 6) and _on("p5") else _noop
            for i, h in enumerate(hs):
                pr, par = h // 2, h % 2
                j = i // 2
                bk = 4 if par == 0 else 0
                wl = [f"{tg}pte"]
                if i == 0:
                    wl += [BREL[4], BREL[0]]
                O("tensor", lambda e, pr=pr, par=par, i=i, j=j, c=c,
                  bk=bk: e.matmul(
                    PB[bk][:, (i // 2) * 128:(i // 2 + 1) * 128],
                    A(eT, pr * 1024 + c * 128 + par * 64 * 8192,
                      [[8192, 64], [1, 128]]),
                    A(sc_pvt, j * 128 + par * 64 * 512,
                      [[512, 64], [1, 128]]),
                    start=True, stop=True),
                  waits=wl, key=f"{tg}wt" if i == 7 else None)
            for i in range(8):
                bk = 4 if i % 2 == 0 else 0
                O("vector", lambda e, i=i, bk=bk: e.tensor_tensor(
                    A(sc_wt, i * 128, [[1024, 128], [1, 128]]),
                    PB[bk][:, (i // 2) * 128:(i // 2 + 1) * 128],
                    c_triu4[:, 0:128], ALU.mult),
                  waits=(f"{tg}wt",) if i == 0 else (),
                  key=f"{tg}wtm" if i == 7 else None)
            BREL[4] = f"{tg}wtm"
            BREL[0] = f"{tg}wtm"
            for vh in range(2):
                O = p.op if (_lim >= 9 or p5s >= 7) and _on("p5") else _noop
                for i4 in range(4):
                    i = vh * 4 + i4
                    h = hs[i]
                    pr, par = h // 2, h % 2
                    j = i // 2
                    wl = [f"{tg}wtm", f"Pvev{c}_{h // 4}"]
                    if i4 == 0:
                        wl.append(BREL[5])
                    O("tensor", lambda e, h=h, i=i, i4=i4, c=c: e.matmul(
                        PB[5][:, i4 * 128:(i4 + 1) * 128],
                        A(sc_wt, i * 128, [[1024, 128], [1, 128]]),
                        A(v_nat, c * 2048 + h * 128, [[16384, 128], [1, 128]]),
                        start=True, stop=False),
                      waits=wl)
                    O("tensor", lambda e, h=h, pr=pr, par=par, i=i, i4=i4,
                      j=j: e.matmul(
                        PB[5][:, i4 * 128:(i4 + 1) * 128],
                        A(sc_pvt, j * 128 + par * 64 * 512,
                          [[512, 64], [1, 128]]),
                        A(st_kv_bf, pr * 192 + 64 + par * 64 * 1536,
                          [[1536, 64], [1, 128]]),
                        start=False, stop=True),
                      key=f"{tg}ov{vh}" if i4 == 3 else None)
                O = p.op if (_lim >= 9 or p5s >= 8) and _on("p5") else _noop
                for i4 in range(4):
                    O("scalar", lambda e, i4=i4: e.activation(
                        sc_osq[:, i4 * 128:(i4 + 1) * 128],
                        PB[5][:, i4 * 128:(i4 + 1) * 128], AF.Square,
                        accum_out=sc_h5[:, i4:i4 + 1]),
                      waits=(f"{tg}ov{vh}",), key=f"{tg}sm{vh}_{i4}")
                O("scalar", lambda e: e.activation(
                    sc_h6[:, 0:4], sc_h5[:, 0:4], AF.Ln, scale=1.0 / DV,
                    bias=c_bias[:, 1:2]), waits=(f"{tg}sm{vh}_3", "bias_c"))
                O("scalar", lambda e: e.activation(
                    sc_h6[:, 4:8], sc_h6[:, 0:4], AF.Exp, scale=-0.5),
                  key=f"{tg}rstd{vh}")
                for i4 in range(4):
                    h = hs[vh * 4 + i4]
                    O("vector", lambda e, i4=i4, h=h, c=c:
                      e.scalar_tensor_tensor(
                        sc_of[:, i4 * 128:(i4 + 1) * 128],
                        PB[5][:, i4 * 128:(i4 + 1) * 128],
                        sc_h6[:, 4 + i4:5 + i4],
                        A(silug, c * 2048 + h * 128, [[16384, 128], [1, 128]]),
                        ALU.mult, ALU.mult),
                      waits=(f"{tg}rstd{vh}", f"Pgev{c}_{h // 4}"),
                      key=f"{tg}of{vh}" if i4 == 3 else None)
                BREL[5] = f"{tg}of{vh}"
                O = p.op if (_lim >= 9 or p5s >= 9) and _on("p5") else _noop
                for i4 in range(4):
                    wl = [f"{tg}of{vh}"]
                    if i4 == 0:
                        wl.append(BREL[3] if vh == 0 else f"{tg}ote0")
                    O("tensor", lambda e, i4=i4, vh=vh: e.transpose(
                        PB[3][:, :].bitcast(BF16)
                        [:, (vh * 4 + i4) * 128:(vh * 4 + i4 + 1) * 128],
                        sc_of[:, i4 * 128:(i4 + 1) * 128], c_id_bf[:, :]),
                      waits=wl, key=f"{tg}ot{vh}" if i4 == 3 else None)
                for i4 in range(4):
                    h = hs[vh * 4 + i4]
                    wlo = [f"{tg}ot{vh}"] if i4 == 0 else []
                    if vh == 0 and i4 == 0 and dg("p1") and c == 0 and G == 0:
                        wlo.append("dmp_d_xlnT")
                    O("vector", lambda e, h=h, c=c, i4=i4, vh=vh:
                      e.tensor_copy(
                        A(oT, h * 1024 + c * 128, [[16384, 128], [1, 128]]),
                        PB[3][:, :].bitcast(BF16)
                        [:, (vh * 4 + i4) * 128:(vh * 4 + i4 + 1) * 128]),
                      waits=wlo,
                      key=f"{tg}ote{vh}" if i4 == 3 else None)
            BREL[3] = f"{tg}ote1"
            O = p.op if (_lim >= 9 or p5s >= 10) and _on("p5") else _noop
            for i, h in enumerate(hs):
                pr, par = h // 2, h % 2
                j = i // 2
                tp = (0, 64) if par else None
                wl = [ek_e[h // 8], ek_k[h // 8]]
                if i == 0:
                    wl.append(BREL[6])
                O("tensor", lambda e, h=h, j=j, tp=tp, par=par: e.matmul(
                    PB[6][par * 64:par * 64 + 64, j * 64:(j + 1) * 64],
                    sc_k[:, h * 64:(h + 1) * 64],
                    sc_e[:, h * 64:(h + 1) * 64],
                    start=True, stop=True, tile_position=tp),
                  waits=wl, key=f"{tg}hk" if i == 7 else None)
                wl2 = [f"Pvev{c}_{h // 4}"]
                if i == 0:
                    wl2.append(BREL[7])
                O("tensor", lambda e, h=h, j=j, tp=tp, par=par, c=c: e.matmul(
                    PB[7][par * 64:par * 64 + 64, j * 128:(j + 1) * 128],
                    sc_e[:, h * 64:(h + 1) * 64],
                    A(v_nat, c * 2048 + h * 128, [[16384, 128], [1, 128]]),
                    start=True, stop=True, tile_position=tp),
                  waits=wl2, key=f"{tg}hv" if i == 7 else None)
            for j in range(4):
                O("vector", lambda e, G=G, j=j: e.tensor_tensor(
                    A(st_kv, (4 * G + j) * 192, [[1536, 128], [1, 64]]),
                    A(st_kv, (4 * G + j) * 192, [[1536, 128], [1, 64]]),
                    PB[6][:, j * 64:(j + 1) * 64], ALU.add),
                  waits=(f"{tg}hk", stkey[("kv", prs[0])]) if j == 0 else (),
                  key=f"{tg}sk" if j == 3 else None)
            for j in range(4):
                O("vector", lambda e, G=G, j=j: e.tensor_tensor(
                    A(st_kv, (4 * G + j) * 192 + 64, [[1536, 128], [1, 128]]),
                    A(st_kv, (4 * G + j) * 192 + 64, [[1536, 128], [1, 128]]),
                    PB[7][:, j * 128:(j + 1) * 128], ALU.add),
                  waits=(f"{tg}hv",) if j == 0 else (),
                  key=f"{tg}sv" if j == 3 else None)
            BREL[6] = f"{tg}sk"
            BREL[7] = f"{tg}sv"
            O("vector", lambda e, G=G: e.tensor_copy(
                A(st_kv_bf, 4 * G * 192, [[1536, 128], [1, 768]]),
                A(st_kv, 4 * G * 192, [[1536, 128], [1, 768]])),
              key=f"{tg}sbf")
            for pr in prs:
                stkey[("kv", pr)] = f"{tg}sbf"
            if dg("p5") and c == 0 and G == 0:
                prevd = None
                for nm, buf, wk in [("d_sS", sc_sS, "c0G0sev"),
                                    ("d_okn", sc_okn, "c0G0okn"),
                                    ("d_pvn", sc_pvn, "c0G0pvn"),
                                    ("d_of", sc_of, "c0G0of1")]:
                    O("sync", lambda e, nm=nm, buf=buf: e.dma_start(
                        out=dbgd[nm], in_=buf[:, :]), waits=(wk, prevd),
                      key=f"dmp_{nm}", dma=True, ch="dbg")
                    prevd = f"dmp_{nm}"
                dbg_gate = [f"dmp_{nm}" for nm in
                            ("d_sS", "d_okn", "d_pvn", "d_of")]

    if dg("ot") and p5c == NCH and p5g >= 2:
        O("sync", lambda e: e.dma_start(out=dbgd["d_oT"], in_=oT[:, :]),
          waits=(f"c{NCH-1}G1ote1",), dma=True, ch="dbg2", key="dmp_d_oT")
    elif dg("ot"):
        p.keyval["dmp_d_oT"] = ("tensor.c", 0)

    O = _gate("p6")
    ld_ds_w = [f"P0tp{NCH-1}_3"]
    if (not dbg) or dbgcc:
        ld_ds_w += ["P4s3", "P4bf"]
    if _on("p5") and p5c == NCH and p5g >= 2 and p5s >= 8:
        ld_ds_w.append(f"c{NCH-1}G1of1")
    O("sync", lambda e: e.dma_start(
        out=c_dsrep[0:1, :],
        in_=A(downs.tensor, 0, [[2048, 1], [1, 2048]])),
      waits=tuple(ld_ds_w), key="ld_ds", dma=True, ch="cst")
    for j in range(4):
        O("tensor", lambda e, j=j: e.matmul(
            PB[0][:, :], c_ones_f32[0:1, :],
            c_dsrep[0:1, j * 512:(j + 1) * 512], start=True, stop=True),
          waits=("ld_ds", BREL[0]), key=f"dsb{j}")
        O("vector", lambda e, j=j: e.tensor_copy(
            c_dsb[:, j * 512:(j + 1) * 512], PB[0][:, :]),
          waits=(f"dsb{j}",), key=f"dsbe{j}")
        BREL[0] = f"dsbe{j}"

    woi = {"i": 0}
    p6ev = {}
    for cg in range(2):
        for nh in range(2):
            for kt in range(NKT):
                i = woi["i"]
                woi["i"] += 1
                O("gpsimd", lambda e, kt=kt, nh=nh, i=i: e.dma_start(
                    out=wo_ring[i % 4][:, :],
                    in_=wts["wo"][kt * 128:(kt + 1) * 128,
                                  nh * 1024:(nh + 1) * 1024]),
                  waits=(f"womm_{i-4}",) if i >= 4 else
                  ((f"c{NCH-1}G1ote1",)
                   + (("dmp_d_oT",) if dg("ot") and "wod" in ot_sites
                      else ())),
                  key=f"wod_{i}", dma=True, ch=f"wo{i % 4}")
                for ci in range(4):
                    for n2 in range(2):
                        b = ci * 2 + n2
                        ch = cg * 4 + ci
                        wl = [f"wod_{i}", f"c{ch}G1ote1"]
                        if kt == 0:
                            wl.append(BREL[b])
                        last = (ci == 3 and n2 == 1)
                        O("tensor", lambda e, ch=ch, kt=kt, n2=n2, b=b, i=i:
                          e.matmul(
                            PB[b][:, :],
                            A(oT, kt * 1024 + ch * 128,
                                    [[16384, 128], [1, 128]]),
                            wo_ring[i % 4][:, n2 * 512:(n2 + 1) * 512],
                            start=(kt == 0), stop=(kt == NKT - 1)),
                          waits=wl, key=f"womm_{i}" if last else None)
            mmk = f"womm_{woi['i'] - 1}"
            for ci in range(4):
                ch = cg * 4 + ci
                wlx = [f"c{NCH-1}G1ote1"]
                if dg("ot") and "p6x" in ot_sites:
                    wlx.append("dmp_d_oT")
                if nh == 1:
                    wlx.append(f"p6a{cg}_0_{ci}_1")
                if cg:
                    wlx += [f"p6w{cg-1}_3", f"P6Lap{(cg-1)*4+ci}"]
                O("sync", lambda e, ch=ch, ci=ci, nh=nh: e.dma_start(
                    out=p6_x[ci][:, :],
                    in_=x_in[ch * 128:(ch + 1) * 128,
                             nh * 1024:(nh + 1) * 1024]),
                  waits=wlx, key=f"p6x{cg}_{nh}_{ci}", dma=True,
                  ch=f"p6x{ci}")
                for n2 in range(2):
                    b = ci * 2 + n2
                    ek = f"p6a{cg}_{nh}_{ci}_{n2}"
                    O("vector", lambda e, ci=ci, nh=nh, n2=n2, b=b: e.tensor_tensor(
                        p6_x1[ci][:, nh * 1024 + n2 * 512:
                                  nh * 1024 + (n2 + 1) * 512],
                        p6_x[ci][:, n2 * 512:(n2 + 1) * 512],
                        PB[b][:, :], ALU.add),
                      waits=(mmk, f"p6x{cg}_{nh}_{ci}"), key=ek)
                    BREL[b] = ek
        for ci in range(4):
            ch = cg * 4 + ci
            src_key = f"p6a{cg}_1_{ci}_1"
            O("sync", lambda e, ci=ci, ch=ch: e.dma_start(
                out=x1s[ch * 128:(ch + 1) * 128, :], in_=p6_x1[ci][:, :]),
              waits=(src_key, f"p6a{cg}_0_{ci}_1"), key=f"p6w{cg}_{ci}",
              dma=True, ch="p6w")
            prev_gate = f"P6Lev{ch-2}_3" if ch >= 2 else "dsbe3"
            emit_ln_chunk(ch, "P6L", src_key, p6_x1[ci][:, :], x2T, 1024,
                          prev_gate,
                          xbuf=(sc_xln if ch % 2 == 0 else xln2b))
    if dg("x2t"):
        O("sync", lambda e: e.dma_start(out=dbgd["d_x2T"], in_=x2T[:, :]),
          waits=(f"P6Lev{NCH-1}_3",), dma=True, ch="dbg2", key="dmp_d_x2T")

    O = _gate("p7")
    x2k = f"P6Lev{NCH-1}_3"
    for ct in range(NCT):
        pb0 = 4 * (ct % 2)
        for w_ap, rj in ((wts["gatew"], 0), (wts["upw"], 1)):
            O("gpsimd", lambda e, ct=ct, w=w_ap, rj=rj: e.dma_start(
                out=r8[2 * (ct % 2) + rj][:, :],
                in_=A(w.tensor, ct * 128,
                            [[FI, 128], [FI * 128, 16], [1, 128]])),
              waits=(f"P7f{ct-2}_1",) if ct >= 2 else (x2k,),
              key=f"P7d{ct}_{rj}", dma=True, ch=f"p7{2 * (ct % 2) + rj}")
        for tg in range(2):
            for rj, nm in ((0, "g"), (1, "u")):
                b = pb0 + rj * 2 + tg
                for kt in range(NKT):
                    wl = [f"P7d{ct}_{rj}", x2k]
                    if kt == 0:
                        wl.append(BREL[b])
                    O("tensor", lambda e, ct=ct, kt=kt, tg=tg, rj=rj, b=b:
                      e.matmul(
                        PB[b][:, :],
                        A(r8[2 * (ct % 2) + rj], kt * 128,
                                [[2048, 128], [1, 128]]),
                        A(x2T, kt * 1024 + tg * 512,
                                [[16384, 128], [1, 512]]),
                        start=(kt == 0), stop=(kt == NKT - 1)),
                      waits=wl,
                      key=f"P7m{ct}_{tg}_{nm}" if kt == NKT - 1 else None)
        for tg in range(2):
            bg = pb0 + 0 * 2 + tg
            bu = pb0 + 1 * 2 + tg
            O("scalar", lambda e, ct=ct, bg=bg: e.activation(
                sg_sc[:, :], PB[bg][:, :], AF.Silu, scale=c_gs[:, ct:ct + 1]),
              waits=(f"P7m{ct}_{tg}_g", "ld_gs")
              + ((f"P7f{ct}_0",) if tg == 1 else
                 ((f"P7f{ct-1}_1",) if ct >= 1 else ())),
              key=f"P7s{ct}_{tg}")
            ek = f"P7f{ct}_{tg}"
            O("vector", lambda e, ct=ct, tg=tg, bu=bu: e.scalar_tensor_tensor(
                A(fuseT, ct * 1024 + tg * 512, [[49152, 128], [1, 512]]),
                PB[bu][:, :], c_us[:, ct:ct + 1], sg_sc[:, :],
                ALU.mult, ALU.mult),
              waits=(f"P7m{ct}_{tg}_u", f"P7s{ct}_{tg}", "ld_us")
              + (("dmp_d_oT",) if dg("ot") and ct == 0
                 and "p7" in ot_sites else ()), key=ek)
            BREL[pb0 + tg] = ek
            BREL[pb0 + 2 + tg] = ek
    if dg("ft"):
        O("sync", lambda e: e.dma_start(out=dbgd["d_fuseT"], in_=fuseT[:, :]),
          waits=(f"P7f{NCT-1}_1",), dma=True, ch="dbg2")

    O = _gate("p8")
    for q in range(4):
        O("scalar", lambda e, q=q: e.dma_start(
            out=down8[:, :].rearrange("p (r f) -> p r f", f=512),
            in_=A(down8s.tensor, q * 512,
                        [[2048, 128], [2048 * 128, 48], [1, 512]])),
          waits=(f"d8_{NCT-1}",)
          + ((f"P8o{q-1}_{NCH-1}",) if q >= 1 else
             (f"P7m{NCT-1}_1_u", f"P7f{NCT-1}_1"))
          + (("dmp_d_x2T",) if dg("x2t") and q == 0 else ()),
          key=f"P8d{q}", dma=True, ch="p8d")
        for tc in range(NCH):
            b = tc % 2
            wlx = ["p6w1_3"]
            if tc >= 2:
                wlx.append(f"P8a{q}_{tc-2}")
            elif q >= 1:
                wlx.append(f"P8a{q-1}_{NCH-2+tc}")
            O("sync", lambda e, q=q, tc=tc: e.dma_start(
                out=p8_x1[tc % 2][:, :],
                in_=x1s[tc * 128:(tc + 1) * 128, q * 512:(q + 1) * 512]),
              waits=wlx, key=f"P8x{q}_{tc}", dma=True, ch=f"p8x{tc % 2}")
            for ct in range(NCT):
                wl = [f"P8d{q}", f"P7f{ct}_{0 if tc < 4 else 1}"]
                if ct == 0:
                    wl.append(BREL[b])
                O("tensor", lambda e, q=q, tc=tc, ct=ct, b=b: e.matmul(
                    PB[b][:, :],
                    A(fuseT, ct * 1024 + tc * 128,
                            [[49152, 128], [1, 128]]),
                    A(down8, ct * 512, [[24576, 128], [1, 512]]),
                    start=(ct == 0), stop=(ct == NCT - 1)),
                  waits=wl, key=f"P8m{q}_{tc}" if ct == NCT - 1 else None)
            wls = [f"P8m{q}_{tc}", "dsbe3"]
            if dg("agg") and q == 0 and tc < 2:
                wls.append("dmp_d_agg")
            if tc >= 2:
                wls.append(f"P8o{q}_{tc-2}")
            elif q >= 1:
                wls.append(f"P8o{q-1}_{NCH-2+tc}")
            O("vector", lambda e, q=q, tc=tc, b=b: e.tensor_tensor(
                p8_o[tc % 2][:, :], PB[b][:, :],
                c_dsb[:, q * 512:(q + 1) * 512], ALU.mult),
              waits=wls, key=f"P8s{q}_{tc}")
            BREL[b] = f"P8s{q}_{tc}"
            O("vector", lambda e, q=q, tc=tc: e.tensor_tensor(
                p8_o[tc % 2][:, :], p8_o[tc % 2][:, :], p8_x1[tc % 2][:, :],
                ALU.add),
              waits=(f"P8x{q}_{tc}",), key=f"P8a{q}_{tc}")
            O("sync", lambda e, q=q, tc=tc: e.dma_start(
                out=out[tc * 128:(tc + 1) * 128, q * 512:(q + 1) * 512],
                in_=p8_o[tc % 2][:, :]),
              waits=(f"P8a{q}_{tc}",), key=f"P8o{q}_{tc}", dma=True,
              ch=f"p8o{tc % 2}")

    with nc.Block("main") as blk:
        p.emit(blk)

        @blk.sync
        def _(e):
            for cn, c in p.cnt.items():
                if cn.startswith("sync.q"):
                    e.wait_ge(c["sem"], c["v"])

        @blk.gpsimd
        def _(e):
            for cn, c in p.cnt.items():
                if cn.startswith("gpsimd"):
                    e.wait_ge(c["sem"], c["v"])

        @blk.scalar
        def _(e):
            for cn, c in p.cnt.items():
                if cn.startswith("scalar.q"):
                    e.wait_ge(c["sem"], c["v"])

    return nc


_NC_CACHE = {}


def kernel(L0_attn_hidden_states, L0_attn_Wq, L0_attn_Wk, L0_attn_Wv,
           L0_attn_Wg, L0_attn_Ws, L0_attn_Wo, L0_attn_g_norm_weight,
           L0_ffn_gate_w, L0_ffn_up_w, L0_ffn_down_w, L0_ffn_gate_scale,
           L0_ffn_up_scale, L0_ffn_down_scale):
    x = np.ascontiguousarray(np.asarray(L0_attn_hidden_states, np.float32))
    B, T, _ = x.shape
    shared = {
        "wq": np.ascontiguousarray(L0_attn_Wq, np.float32),
        "wk": np.ascontiguousarray(L0_attn_Wk, np.float32),
        "ws": np.ascontiguousarray(L0_attn_Ws, np.float32),
        "wv": np.ascontiguousarray(L0_attn_Wv, np.float32),
        "wg": np.ascontiguousarray(L0_attn_Wg, np.float32),
        "wo": np.ascontiguousarray(L0_attn_Wo, np.float32),
        "gnw": np.ascontiguousarray(L0_attn_g_norm_weight, np.float32),
        "gatew": np.ascontiguousarray(L0_ffn_gate_w, np.float32),
        "upw": np.ascontiguousarray(L0_ffn_up_w, np.float32),
        "downw": np.ascontiguousarray(L0_ffn_down_w, np.float32),
        "gates": np.ascontiguousarray(L0_ffn_gate_scale, np.float32),
        "ups": np.ascontiguousarray(L0_ffn_up_scale, np.float32),
        "downs": np.ascontiguousarray(L0_ffn_down_scale, np.float32),
    }
    in_maps = []
    for core in range(8):
        b, j = core // 4, core % 4
        pm = np.zeros((128, 4), np.float32)
        pm[:, :j] = 1.0
        m = dict(shared)
        m["x"] = np.ascontiguousarray(x[b, j * TS:(j + 1) * TS, :])
        m["pmask"] = pm
        in_maps.append(m)
    if "nc" not in _NC_CACHE:
        _NC_CACHE["nc"] = build(dbg=False)
    nc = _NC_CACHE["nc"]
    res = run_bass_kernel_spmd(nc, in_maps, core_ids=list(range(8)))
    outs = [np.asarray(res.results[c]["out"]) for c in range(8)]
    full = np.concatenate(outs, axis=0).reshape(B, T, D)
    return full
